# version 01
# baseline (speedup 1.0000x reference)
import contextlib
import numpy as np
import ml_dtypes

import concourse.bass as bass
import concourse.mybir as mybir
from concourse.bass_utils import run_bass_kernel_spmd

F32 = mybir.dt.float32
BF16 = mybir.dt.bfloat16
AF = mybir.ActivationFunctionType
ALU = mybir.AluOpType

B, S, D = 32, 2048, 1024
NCORES = 8
SEQ_PER_CORE = B // NCORES
NCH = D // 128
TT = 512
NTILES = S // TT
NSUB = TT // 128
H = 16
XL = 520
EPS = 1e-5

SUBLAYERS = (0, 1, 2, 3)
DBG = False
ATTNB_STAGE = 4


class Ctr:
    __slots__ = ("h", "n")

    def __init__(self, handle):
        self.h = handle
        self.n = 0


class Prog:
    def __init__(self, ctrs):
        self.streams = {e: [] for e in ("tensor", "vector", "scalar", "sync", "gpsimd")}
        self.c = ctrs
        self.last_wait = {}

    def op(self, eng, fn, inc=None, k=1):
        if inc is None:
            self.streams[eng].append(fn)
            return None
        ctr = self.c[inc]
        ctr.n += k
        h = ctr.h
        self.streams[eng].append(lambda e, fn=fn, h=h, k=k: fn(e).then_inc(h, k))
        return ctr.n

    def wait(self, eng, cname, val=None):
        ctr = self.c[cname]
        v = ctr.n if val is None else val
        if v <= 0:
            return
        if eng == cname:
            h = ctr.h
            self.streams[eng].append(lambda e, h=h, v=v: e.wait_ge(h, v))
            return
        key = (eng, cname)
        if self.last_wait.get(key, -1) >= v:
            return
        self.last_wait[key] = v
        h = ctr.h
        self.streams[eng].append(lambda e, h=h, v=v: e.wait_ge(h, v))


_CACHE = {}


def build_program():
    if "nc" in _CACHE:
        return _CACHE["nc"]
    nc = bass.Bass("TRN2", target_bir_lowering=False)
    es = contextlib.ExitStack()
    A = es.enter_context

    d_x = nc.dram_tensor("x", [SEQ_PER_CORE, D, S], F32, kind="ExternalInput").ap()
    d_out = nc.dram_tensor("out", [SEQ_PER_CORE, D, S], F32, kind="ExternalOutput").ap()
    dw = {}
    for nm in ("wq", "wk", "wv", "wo", "wr", "wfk", "wfv", "wproj"):
        dw[nm] = nc.dram_tensor(nm, [D, D], F32, kind="ExternalInput").ap()
    dw["wqkv"] = nc.dram_tensor("wqkv", [D, 3 * D], F32, kind="ExternalInput").ap()
    d_wfbdk = nc.dram_tensor("wfbdk", [128, 512], F32, kind="ExternalInput").ap()
    d_wfbdq = nc.dram_tensor("wfbdq", [128, 512], F32, kind="ExternalInput").ap()
    d_mu = nc.dram_tensor("mu", [128, 48], F32, kind="ExternalInput").ap()
    d_ident = nc.dram_tensor("ident", [128, 128], BF16, kind="ExternalInput").ap()
    d_m01 = nc.dram_tensor("m01", [128, 128], BF16, kind="ExternalInput").ap()
    d_ones = nc.dram_tensor("onesin", [128, 4], F32, kind="ExternalInput").ap()
    d_sel16 = nc.dram_tensor("sel16", [16, 1024], F32, kind="ExternalInput").ap()

    s_x = A(nc.sbuf_tensor("s_x", [128, NCH * S], F32))
    s_xlnb = A(nc.sbuf_tensor("s_xlnb", [128, NCH * XL], BF16))
    s_ar2 = A(nc.sbuf_tensor("s_ar2", [128, 16384], BF16))
    s_scr = A(nc.sbuf_tensor("s_scr", [128, 16384], BF16))
    s_w = [A(nc.sbuf_tensor(f"s_w{i}", [128, 8192], BF16)) for i in range(3)]
    s_wfbdk = A(nc.sbuf_tensor("s_wfbdk", [128, 512], BF16))
    s_wfbdq = A(nc.sbuf_tensor("s_wfbdq", [128, 512], BF16))
    s_mu = A(nc.sbuf_tensor("s_mu", [128, 48], F32))
    s_ident = A(nc.sbuf_tensor("s_ident", [128, 128], BF16))
    s_m01 = A(nc.sbuf_tensor("s_m01", [128, 128], BF16))
    s_onesb = A(nc.sbuf_tensor("s_onesb", [128, 4], BF16))
    s_onesrow = A(nc.sbuf_tensor("s_onesrow", [1, 128], F32))
    s_sel16 = A(nc.sbuf_tensor("s_sel16", [16, 1024], F32))
    s_xbc = A(nc.sbuf_tensor("s_xbc", [128, 2048], BF16))
    xbcF = s_xbc[:, :].bitcast(F32)
    s_lntmp = xbcF[:, 0:512]
    s_rowA = xbcF[0:1, 0:512]
    s_rowC = xbcF[0:1, 512:1024]
    s_rn16 = s_scr[0:16, 14848:15872].bitcast(F32)
    s_rn8 = A(nc.sbuf_tensor("s_rn8", [128, 8], F32))
    s_cst = A(nc.sbuf_tensor("s_cst", [128, 4], F32))
    s_expb = A(nc.sbuf_tensor("s_expb", [128, 1024], BF16))
    s_v8 = A(nc.sbuf_tensor("s_v8", [128, 8 * 65], BF16))
    s_outx = A(nc.sbuf_tensor("s_outx", [128, 8 * 64], BF16))
    s_low = A(nc.sbuf_tensor("s_low", [128, 4096], BF16))

    PA = A(nc.psum_tensor("p_a", [128, 512], F32))
    PB = A(nc.psum_tensor("p_b", [128, 512], F32))
    KVA = A(nc.psum_tensor("p_kva", [128, 1024], F32))
    KVB = A(nc.psum_tensor("p_kvb", [128, 1024], F32))
    KVC = A(nc.psum_tensor("p_kvc", [128, 1024], F32))

    ctrs = {n: Ctr(A(nc.semaphore("c_" + n))) for n in ("t", "v", "a", "s", "g")}
    block = A(nc.Block())
    p = Prog(ctrs)

    def kv_slice(sl):
        t = KVA if sl < 16 else KVB
        o = 64 * (sl % 16)
        return t[:, o:o + 64]

    oKT, oKF, oVT = 0, 4096, 8192
    oQF, oO, oBD, oNT = 0, 8192, 10240, 14336
    oXR, oXK, oKA = 0, 4096, 8192
    oQ, oK, oV = 0, 4096, 8192

    def gdma(dst, src):
        p.op("gpsimd", lambda e, d=dst, s_=src: e.dma_start(out=d, in_=s_), inc="g", k=16)

    def sdma(dst, src):
        p.op("sync", lambda e, d=dst, s_=src: e.dma_start(out=d, in_=s_), inc="s", k=16)

    def mm(out, lhsT, rhs, start, stop, inc=None, tile_position=None):
        p.op("tensor",
             lambda e, o=out, l=lhsT, r=rhs, st=start, sp=stop, tp=tile_position:
             e.matmul(o, l, r, start=st, stop=sp, tile_position=tp,
                      skip_group_check=True),
             inc=inc)

    def ttrans(out, in_, ident, inc=None, tile_position=None):
        p.op("tensor",
             lambda e, o=out, i=in_, d=ident, tp=tile_position:
             e.transpose(o, i, d, tile_position=tp),
             inc=inc)

    def vop(fn):
        p.op("vector", fn, inc="v")

    def aop(fn):
        p.op("scalar", fn, inc="a")

    def tmark():
        mm(PA[0:1, 0:1], s_onesb[0:1, 0:1], s_onesb[0:1, 0:1], True, True, inc="t")
        return ctrs["t"].n

    def barrier():
        tn = tmark()
        vn, an = ctrs["v"].n, ctrs["a"].n
        p.wait("tensor", "v", vn)
        p.wait("tensor", "a", an)
        p.wait("vector", "t", tn)
        p.wait("vector", "a", an)
        p.wait("scalar", "t", tn)
        p.wait("scalar", "v", vn)

    gdma(s_mu[:], d_mu[:])
    gdma(s_ident[:], d_ident[:])
    gdma(s_m01[:], d_m01[:])
    gdma(s_onesb[:], d_ones[:])
    gdma(s_sel16[:], d_sel16[:])
    gdma(s_wfbdk[:], d_wfbdk[:])
    gdma(s_wfbdq[:], d_wfbdq[:])
    g_const = ctrs["g"].n
    p.wait("vector", "g", g_const)
    vop(lambda e: e.memset(s_onesrow[:], 1.0))
    vop(lambda e: e.memset(s_cst[:, 0:1], -10.0))
    vop(lambda e: e.memset(s_cst[:, 1:2], EPS))
    vop(lambda e: e.memset(
        s_v8.rearrange("p (g c) -> p g c", g=8)[:, :, 64:65], 1.0))
    ONESB = s_onesb[:, 0:1]

    slot_loaded = [0, 0, 0]
    slot_freed = [0, 0, 0]

    def load_weight(slot, dram_ap):
        p.wait("gpsimd", "t", slot_freed[slot])
        src = dram_ap.rearrange("(c p) f -> p c f", p=128)
        dst = s_w[slot].rearrange("p (c f) -> p c f", c=8)
        gdma(dst, src)
        slot_loaded[slot] = ctrs["g"].n

    def free_slots():
        n = ctrs["t"].n
        slot_freed[0] = slot_freed[1] = slot_freed[2] = n

    def wtile(slot, c, m):
        return s_w[slot][:, 1024 * c + 128 * m: 1024 * c + 128 * m + 128]

    def emit_ln(tile):
        t0 = tile * TT
        marks = {}
        for c in range(NCH):
            pp = 1024 * (c % 2)
            xb = s_xbc[:, pp: pp + 512]
            xq = s_xbc[:, pp + 512: pp + 1024]
            if c >= 2:
                p.wait("vector", "t", marks[c - 2])
            xs = s_x[:, S * c + t0: S * c + t0 + TT]
            vop(lambda e, xb=xb, xs=xs: e.tensor_copy(out=xb, in_=xs))
            vop(lambda e, xq=xq, xb=xb: e.tensor_mul(xq, xb, xb))
            p.wait("tensor", "v")
            mm(PA[0:1, :], ONESB, xb, start=(c == 0), stop=(c == NCH - 1))
            mm(PB[0:1, :], ONESB, xq, start=(c == 0), stop=(c == NCH - 1), inc="t")
            marks[c] = ctrs["t"].n
        p.wait("vector", "t")
        rA, rC = s_rowA, s_rowC
        vop(lambda e: e.tensor_scalar_mul(rA, PA[0:1, :], 1.0 / D))
        vop(lambda e: e.tensor_mul(rC, rA, rA))
        vop(lambda e: e.scalar_tensor_tensor(out=PA[0:1, :], in0=PB[0:1, :],
                                             scalar=1.0 / D, in1=rC,
                                             op0=ALU.mult, op1=ALU.subtract))
        p.wait("scalar", "v")
        aop(lambda e: e.activation(out=rC, in_=PA[0:1, :], func=AF.Ln,
                                   bias=s_cst[0:1, 1:2]))
        aop(lambda e: e.activation(out=rC, in_=rC, func=AF.Exp, scale=-0.5))
        p.wait("vector", "a")
        vop(lambda e: e.tensor_mul(rA, rA, rC))
        p.wait("tensor", "v")
        p.wait("tensor", "a")
        mm(PA[:], s_onesrow[:], rC, start=True, stop=True)
        mm(PB[:], s_onesrow[:], rA, start=True, stop=True, inc="t")
        p.wait("vector", "t")
        for c in range(NCH):
            xs = s_x[:, S * c + t0: S * c + t0 + TT]
            dst = s_xlnb[:, XL * c + 1: XL * c + 1 + TT]
            vop(lambda e, xs=xs: e.tensor_mul(s_lntmp, xs, PA[:]))
            vop(lambda e, dst=dst: e.tensor_sub(dst, s_lntmp, PB[:]))

    def xlnb_c(c, lo, n):
        return s_xlnb[:, XL * c + 1 + lo: XL * c + 1 + lo + n]

    def emit_proj(slot, rhs_fn, evac, w=512):
        for m in range(NCH):
            ps = (PA if m % 2 == 0 else PB)[:, 0:w]
            p.wait("tensor", "v")
            p.wait("tensor", "a")
            for c in range(NCH):
                mm(ps, wtile(slot, c, m), rhs_fn(c), start=(c == 0), stop=(c == NCH - 1),
                   inc=("t" if c == NCH - 1 else None))
            evac(m, ps)

    def elu_evac(ps, dst, w=512):
        p.wait("vector", "t")
        t1 = s_xbc[:, 0:w]
        t3 = s_xbc[:, 1024:1024 + w]
        vop(lambda e, t1=t1, ps=ps: e.tensor_scalar_min(t1, ps, 0.0))
        vop(lambda e, t3=t3, ps=ps: e.tensor_scalar_max(t3, ps, 0.0))
        p.wait("scalar", "v")
        aop(lambda e, t1=t1: e.activation(out=t1, in_=t1, func=AF.Exp))
        p.wait("vector", "a")
        vop(lambda e, dst=dst, t1=t1, t3=t3: e.tensor_add(dst, t1, t3))

    def emit_attn_a():
        QS = s_ar2
        load_weight(0, dw["wq"])
        load_weight(1, dw["wk"])
        load_weight(2, dw["wv"])
        for tile in range(NTILES):
            emit_ln(tile)
            p.wait("tensor", "g", slot_loaded[0])

            def ev_q(m, ps, tile=tile):
                p.wait("vector", "t")
                dst = QS[:, 2048 * m + TT * tile: 2048 * m + TT * tile + TT]
                vop(lambda e, dst=dst, ps=ps: e.tensor_copy(out=dst, in_=ps))
            emit_proj(0, lambda c: xlnb_c(c, 0, TT), ev_q)

            p.wait("tensor", "g", slot_loaded[1])

            def ev_k(m, ps):
                p.wait("vector", "t")
                dst = s_scr[:, oKT + 512 * m: oKT + 512 * m + TT]
                vop(lambda e, dst=dst, ps=ps: e.tensor_copy(out=dst, in_=ps))
            emit_proj(1, lambda c: xlnb_c(c, 0, TT), ev_k)
            p.wait("tensor", "g", slot_loaded[2])

            for sub in range(NSUB):
                lo = 128 * sub
                p.wait("tensor", "v")
                for c in range(NCH):
                    lhsT = xlnb_c(c, lo, 128)
                    mm(PA[:], lhsT, s_w[2][:, 1024 * c: 1024 * c + 512],
                       start=(c == 0), stop=(c == NCH - 1))
                    mm(PB[:], lhsT, s_w[2][:, 1024 * c + 512: 1024 * c + 1024],
                       start=(c == 0), stop=(c == NCH - 1),
                       inc=("t" if c == NCH - 1 else None))
                p.wait("vector", "t")
                for half, ps in ((0, PA[:]), (1, PB[:])):
                    dst = s_scr[:, oVT + 512 * half: oVT + 512 * half + 512]
                    vop(lambda e, dst=dst, ps=ps: e.tensor_copy(out=dst, in_=ps))
                for c in range(NCH):
                    ps = (PA if c % 2 == 0 else PB)[:]
                    p.wait("tensor", "v")
                    p.wait("tensor", "a")
                    mm(ps, s_scr[:, oKT + 512 * c + lo: oKT + 512 * c + lo + 128],
                       s_wfbdk[:], start=True, stop=True, inc="t")
                    elu_evac(ps, s_scr[:, oKF + 512 * c: oKF + 512 * c + 512])
                p.wait("tensor", "v")
                first = (tile == 0 and sub == 0)
                last = (tile == NTILES - 1 and sub == NSUB - 1)
                for h in range(H):
                    for ph in range(2):
                        sl = 2 * h + ph
                        o = oKF + 512 * (h // 2) + 256 * (h % 2) + 128 * ph
                        mm(kv_slice(sl), s_scr[:, o:o + 128],
                           s_scr[:, oVT + 64 * h: oVT + 64 * h + 64],
                           start=first, stop=last)
                        mm(KVC[:, 2 * sl:2 * sl + 1], s_scr[:, o:o + 128],
                           ONESB, start=first, stop=last,
                           inc=("t" if sl == 31 else None))
            barrier()

        p.wait("vector", "t")
        for c in range(NCH):
            for j in range(4):
                idx = 4 * c + j
                bd = s_scr[:, oBD + 128 * idx: oBD + 128 * idx + 128]
                ntl = s_scr[:, oNT + 16 * idx: oNT + 16 * idx + 16]
                vop(lambda e, bd=bd: e.memset(bd, 0.0))
                vop(lambda e, ntl=ntl: e.memset(ntl, 0.0))
                for par in range(2):
                    h = 2 * c + par
                    sl = 2 * h + (j // 2)
                    src = kv_slice(sl)[64 * (j % 2):64 * (j % 2) + 64, :]
                    ks = KVC[64 * (j % 2):64 * (j % 2) + 64, 2 * sl:2 * sl + 1]
                    vop(lambda e, bd=bd, src=src, par=par:
                        e.tensor_copy(out=bd[64 * par:64 * par + 64,
                                             64 * par:64 * par + 64],
                                      in_=src))
                    vop(lambda e, ntl=ntl, ks=ks, par=par, h=h:
                        e.tensor_copy(out=ntl[64 * par:64 * par + 64, h:h + 1],
                                      in_=ks))
        barrier()

        free_slots()
        load_weight(0, dw["wo"])
        HT = 256
        for ht in range(S // HT):
            tb = HT * ht
            for c in range(NCH):
                for j in range(4):
                    idx = 4 * c + j
                    ps = (PA if idx % 2 == 0 else PB)[:, 0:HT]
                    p.wait("tensor", "v")
                    p.wait("tensor", "a")
                    mm(ps, s_wfbdq[:, 128 * j:128 * j + 128],
                       QS[:, 2048 * c + tb: 2048 * c + tb + HT],
                       start=True, stop=True, inc="t")
                    elu_evac(ps, s_scr[:, oQF + 256 * idx: oQF + 256 * idx + HT], w=HT)
            p.wait("tensor", "v")
            for idx in range(32):
                mm(KVA[0:16, 0:HT], s_scr[:, oNT + 16 * idx: oNT + 16 * idx + 16],
                   s_scr[:, oQF + 256 * idx: oQF + 256 * idx + HT],
                   start=(idx == 0), stop=(idx == 31),
                   inc=("t" if idx == 31 else None))
            p.wait("vector", "t")
            rn = s_rn16[:, 0:HT]
            vop(lambda e, rn=rn: e.tensor_scalar_add(rn, KVA[0:16, 0:HT], 1e-6))
            vop(lambda e, rn=rn: e.reciprocal(out=rn, in_=rn))
            for c in range(NCH):
                p.wait("tensor", "v")
                mm(KVC[:, 0:HT], s_sel16[:, 128 * c:128 * c + 128], s_rn16[:, 0:HT],
                   start=True, stop=True)
                for j in range(4):
                    idx = 4 * c + j
                    mm(KVB[:, 0:HT], s_scr[:, oBD + 128 * idx: oBD + 128 * idx + 128],
                       s_scr[:, oQF + 256 * idx: oQF + 256 * idx + HT],
                       start=(j == 0), stop=(j == 3), inc=("t" if j == 3 else None))
                p.wait("vector", "t")
                dst = s_scr[:, oO + 256 * c: oO + 256 * c + HT]
                vop(lambda e: e.tensor_copy(out=s_lntmp[:, 0:HT], in_=KVC[:, 0:HT]))
                vop(lambda e, dst=dst: e.tensor_mul(dst, KVB[:, 0:HT],
                                                    s_lntmp[:, 0:HT]))
            p.wait("tensor", "g", slot_loaded[0])

            def ev_o(m, ps, tb=tb):
                p.wait("vector", "t")
                xs = s_x[:, S * m + tb: S * m + tb + HT]
                vop(lambda e, xs=xs, ps=ps: e.tensor_add(xs, xs, ps))
            emit_proj(0, lambda c: s_scr[:, oO + 256 * c: oO + 256 * c + HT],
                      ev_o, w=HT)
            barrier()
        free_slots()

    def emit_ffn():
        load_weight(0, dw["wr"])
        load_weight(1, dw["wfk"])
        load_weight(2, dw["wfv"])
        for tile in range(NTILES):
            emit_ln(tile)
            if tile == 0:
                for c in range(NCH):
                    vop(lambda e, c=c: e.memset(s_xlnb[:, XL * c:XL * c + 1], 0.0))
            for c in range(NCH):
                cur = s_xlnb[:, XL * c + 1: XL * c + 1 + TT]
                prev = s_xlnb[:, XL * c: XL * c + TT]
                for (oDST, mo, omo) in ((oXR, 0, 8), (oXK, 16, 24)):
                    t1 = s_xbc[:, 1024 * (c % 2): 1024 * (c % 2) + 512]
                    dst = s_scr[:, oDST + 512 * c: oDST + 512 * c + 512]
                    vop(lambda e, t1=t1, prev=prev, c=c, o=omo:
                        e.tensor_scalar_mul(t1, prev, s_mu[:, o + c:o + c + 1]))
                    vop(lambda e, dst=dst, cur=cur, t1=t1, c=c, o=mo:
                        e.scalar_tensor_tensor(out=dst, in0=cur,
                                               scalar=s_mu[:, o + c:o + c + 1],
                                               in1=t1, op0=ALU.mult, op1=ALU.add))
            for c in range(NCH):
                vop(lambda e, c=c: e.tensor_copy(
                    out=s_xlnb[:, XL * c:XL * c + 1],
                    in_=s_xlnb[:, XL * c + TT:XL * c + TT + 1]))
            p.wait("tensor", "g", slot_loaded[0])

            def ev_r(m, ps):
                p.wait("scalar", "t")
                dst = s_ar2[:, 1024 * m: 1024 * m + 1024].bitcast(F32)
                aop(lambda e, dst=dst, ps=ps: e.activation(out=dst, in_=ps,
                                                           func=AF.Sigmoid))
            emit_proj(0, lambda c: s_scr[:, oXR + 512 * c: oXR + 512 * c + 512], ev_r)
            p.wait("tensor", "g", slot_loaded[1])

            def ev_k(m, ps):
                p.wait("scalar", "t")
                p.wait("scalar", "v")
                t1 = s_xbc[:, 1024 * (m % 2): 1024 * (m % 2) + 512]
                dst = s_scr[:, oKA + 512 * m: oKA + 512 * m + 512]
                aop(lambda e, t1=t1, ps=ps: e.activation(out=t1, in_=ps, func=AF.Relu))
                p.wait("vector", "a")
                vop(lambda e, dst=dst, t1=t1: e.tensor_mul(dst, t1, t1))
            emit_proj(1, lambda c: s_scr[:, oXK + 512 * c: oXK + 512 * c + 512], ev_k)
            p.wait("tensor", "g", slot_loaded[2])
            t0 = tile * TT

            def ev_v(m, ps, t0=t0):
                p.wait("vector", "t")
                p.wait("vector", "a")
                r = s_ar2[:, 1024 * m: 1024 * m + 1024].bitcast(F32)
                xs = s_x[:, S * m + t0: S * m + t0 + TT]
                vop(lambda e, r=r, ps=ps: e.tensor_mul(s_lntmp, r, ps))
                vop(lambda e, xs=xs: e.tensor_add(xs, xs, s_lntmp))
            emit_proj(2, lambda c: s_scr[:, oKA + 512 * c: oKA + 512 * c + 512], ev_v)
            barrier()
        free_slots()

    def emit_attn_b(seqidx=0):
        OD = s_ar2
        load_weight(0, dw["wqkv"][:, 0:D])
        load_weight(1, dw["wqkv"][:, D:2 * D])
        load_weight(2, dw["wqkv"][:, 2 * D:3 * D])
        qlow = s_low[0:64, 0:2048]
        klow = s_low[0:64, 2048:4096]
        vlow = s_low[64:128, 0:2048]
        for tile in range(NTILES):
            emit_ln(tile)
            p.wait("vector", "s")
            for kslot, oDst in ((0, oQ), (1, oK), (2, oV)):
                p.wait("tensor", "g", slot_loaded[kslot])
                reg = s_scr[:, oDst:oDst + 4096].rearrange("p (g w) -> p g w", g=64)

                def ev(m, ps, reg=reg):
                    p.wait("vector", "t")
                    dst = reg[:, :, 8 * m:8 * m + 8]
                    psv = ps.rearrange("p (g t) -> p g t", g=64)
                    vop(lambda e, dst=dst, psv=psv: e.tensor_copy(out=dst, in_=psv))
                emit_proj(kslot, lambda c: xlnb_c(c, 0, TT), ev)
            for sub in range(NSUB):
                lo = 128 * sub
                p.wait("sync", "v")
                p.wait("sync", "t")
                for oSrc, lowt in ((oQ, qlow), (oK, klow), (oV, vlow)):
                    reg = s_scr[:, oSrc:oSrc + 4096].rearrange("p (g w) -> p g w", g=64)
                    se = reg[0:64, 16 * sub:16 * sub + 16, :]
                    so = reg[64:128, 16 * sub:16 * sub + 16, :]
                    lv = lowt.rearrange("p (g par w) -> p g par w", g=16, par=2)
                    sdma(lv[:, :, 0, :], se)
                    sdma(lv[:, :, 1, :], so)
                p.wait("tensor", "s")
                for gb in range(2):
                    p.wait("tensor", "a")
                    for g8 in range(8):
                        grp = 8 * gb + g8
                        mm(KVC[:, 128 * g8:128 * g8 + 128],
                           klow[:, 128 * grp:128 * grp + 128],
                           qlow[:, 128 * grp:128 * grp + 128],
                           start=True, stop=True, inc=("t" if g8 == 7 else None))
                    p.wait("scalar", "t")
                    p.wait("scalar", "v")
                    aop(lambda e: e.activation(out=s_expb[:], in_=KVC[:, 0:1024],
                                               func=AF.Exp, bias=s_cst[:, 0:1], scale=0.125))
                    p.wait("vector", "a")
                    for g8 in range(8):
                        ev_ = s_expb[:, 128 * g8:128 * g8 + 128]
                        vop(lambda e, ev_=ev_: e.tensor_tensor(
                            out=ev_, in0=ev_, in1=s_m01[:], op=ALU.mult))
                    if ATTNB_STAGE < 2:
                        continue
                    p.wait("tensor", "v")
                    for g8 in range(8):
                        grp = 8 * gb + g8
                        pslot = PA[:, 32 * g8:32 * g8 + 32].bitcast(BF16)
                        ttrans(pslot, vlow[:, 128 * grp:128 * grp + 128],
                               s_ident[64:128, 64:128],
                               tile_position=(64, 0),
                               inc=("t" if g8 == 7 else None))
                    p.wait("scalar", "t")
                    for g8 in range(8):
                        pslot = PA[:, 32 * g8:32 * g8 + 32].bitcast(BF16)
                        dst = s_v8[:, 65 * g8:65 * g8 + 64]
                        aop(lambda e, dst=dst, pslot=pslot: e.activation(
                            out=dst, in_=pslot, func=AF.Copy))
                    if ATTNB_STAGE < 3:
                        continue
                    p.wait("tensor", "a")
                    p.wait("tensor", "v")
                    for g8 in range(8):
                        mm(KVA[:, 128 * g8:128 * g8 + 65],
                           s_expb[:, 128 * g8:128 * g8 + 128],
                           s_v8[:, 65 * g8:65 * g8 + 65],
                           start=True, stop=True, inc=("t" if g8 == 7 else None))
                    p.wait("vector", "t")
                    kvv = KVA[:, 0:1024].rearrange("p (g c) -> p g c", g=8)
                    vop(lambda e, kvv=kvv: e.reciprocal(out=s_rn8[:],
                                                        in_=kvv[:, :, 64:65]))
                    p.wait("vector", "v")
                    for g8 in range(8):
                        ov = s_outx[:, 64 * g8:64 * g8 + 64]
                        kslc = KVA[:, 128 * g8:128 * g8 + 64]
                        vop(lambda e, ov=ov, kslc=kslc, g8=g8: e.tensor_scalar_mul(
                            ov, kslc, s_rn8[:, g8:g8 + 1]))
                    if ATTNB_STAGE < 4:
                        continue
                    p.wait("tensor", "v")
                    for g8 in range(8):
                        pslot = PA[:, 256 + 32 * g8:256 + 32 * g8 + 32].bitcast(BF16)
                        ine = s_outx[0:64, 64 * g8:64 * g8 + 64]
                        ino = s_outx[64:128, 64 * g8:64 * g8 + 64]
                        ttrans(pslot[0:64, :], ine, s_ident[0:64, 0:64])
                        ttrans(pslot[64:128, :], ino, s_ident[64:128, 64:128],
                               tile_position=(64, 64),
                               inc=("t" if g8 == 7 else None))
                    p.wait("scalar", "t")
                    gbase = (tile * TT + lo) // 8 + 8 * gb
                    dst = OD[:, 64 * gbase: 64 * gbase + 512]
                    psrc = PA[:, 256:512].bitcast(BF16)
                    aop(lambda e, dst=dst, psrc=psrc: e.activation(
                        out=dst, in_=psrc, func=AF.Copy))
                    if DBG and seqidx == 0 and tile == 0 and sub == 0 and gb == 0:
                        dbf = s_scr[:, 12288:14336]
                        dbF = dbf.bitcast(F32)
                        p.wait("vector", "t")
                        p.wait("vector", "a")
                        vop(lambda e: e.tensor_copy(out=dbF[:, 0:512],
                                                    in_=KVC[:, 0:512]))
                        vop(lambda e: e.tensor_copy(out=dbF[:, 512:640],
                                                    in_=s_expb[:, 0:128]))
                        vop(lambda e: e.tensor_copy(out=dbF[:, 640:705],
                                                    in_=s_v8[:, 0:65]))
                        vop(lambda e: e.tensor_copy(out=dbF[:, 705:769],
                                                    in_=s_outx[:, 0:64]))
                        vop(lambda e: e.tensor_copy(out=dbF[:, 769:834],
                                                    in_=KVA[:, 0:65]))
                        vop(lambda e: e.tensor_copy(out=dbF[:, 834:898],
                                                    in_=OD[:, 0:64]))
                        p.wait("sync", "v")
                        dv = d_out[0].rearrange("(c p) t -> p c t", p=128)
                        sdma(dv[:, 0, 0:1024], dbF[:, 0:1024])
            barrier()
        if ATTNB_STAGE < 4:
            free_slots()
            return
        free_slots()
        load_weight(0, dw["wproj"])
        p.wait("tensor", "g", slot_loaded[0])
        odv = OD.rearrange("p (G c) -> p G c", G=256)
        for tile in range(NTILES):
            t0 = tile * TT
            for m in range(NCH):
                p.wait("tensor", "v")
                for hp in range(8):
                    le = s_w[0][0:64, 1024 * hp + 128 * m: 1024 * hp + 128 * m + 128]
                    lo_ = s_w[0][64:128, 1024 * hp + 128 * m: 1024 * hp + 128 * m + 128]
                    re = odv[0:64, 64 * tile:64 * tile + 64, 8 * hp:8 * hp + 8]
                    ro = odv[64:128, 64 * tile:64 * tile + 64, 8 * hp:8 * hp + 8]
                    mm(KVB[:, 0:512], le, re, start=(hp == 0), stop=(hp == 7))
                    mm(KVB[:, 512:1024], lo_, ro, start=(hp == 0), stop=(hp == 7),
                       inc=("t" if hp == 7 else None))
                p.wait("vector", "t")
                xs = s_x[:, S * m + t0: S * m + t0 + TT]
                bcol = s_mu[:, 32 + m:33 + m]
                vop(lambda e, xs=xs, bcol=bcol: e.scalar_tensor_tensor(
                    out=s_lntmp, in0=KVB[:, 512:1024], scalar=bcol,
                    in1=xs, op0=ALU.add, op1=ALU.add))
                vop(lambda e, xs=xs: e.tensor_add(xs, s_lntmp, KVB[:, 0:512]))
        barrier()
        free_slots()

    for seq in range(1 if DBG else SEQ_PER_CORE):
        p.wait("sync", "s")
        p.wait("sync", "v")
        xv = d_x[seq].rearrange("(c p) t -> p c t", p=128)
        for c in range(NCH):
            sdma(s_x[:, S * c: S * c + S], xv[:, c, :])
        p.wait("tensor", "s")
        p.wait("vector", "s")
        if 0 in SUBLAYERS:
            emit_attn_a()
        if 1 in SUBLAYERS:
            emit_ffn()
        if 2 in SUBLAYERS:
            emit_attn_b(seq)
        if 3 in SUBLAYERS:
            emit_ffn()
        barrier()
        p.wait("sync", "v")
        if not DBG:
            ov = d_out[seq].rearrange("(c p) t -> p c t", p=128)
            for c in range(NCH):
                sdma(ov[:, c, :], s_x[:, S * c: S * c + S])
    p.wait("sync", "s")
    fin = ctrs["s"].n
    p.wait("gpsimd", "s", fin)

    @block.tensor
    def _(e):
        for fn in p.streams["tensor"]:
            fn(e)

    @block.vector
    def _(e):
        for fn in p.streams["vector"]:
            fn(e)

    @block.scalar
    def _(e):
        for fn in p.streams["scalar"]:
            fn(e)

    @block.sync
    def _(e):
        for fn in p.streams["sync"]:
            fn(e)

    @block.gpsimd
    def _(e):
        for fn in p.streams["gpsimd"]:
            fn(e)

    es.close()
    nc.finalize()
    _CACHE["nc"] = nc
    return nc



def _host_inputs(inputs):
    x = np.asarray(inputs["L0_attn_a_hidden_states"], dtype=np.float32)
    xt = np.ascontiguousarray(x.transpose(0, 2, 1))

    def T(a):
        return np.ascontiguousarray(np.asarray(a, dtype=np.float32).T)

    wq = T(inputs["L0_attn_a_Wq"])
    wk = T(inputs["L0_attn_a_Wk"])
    wv = T(inputs["L0_attn_a_Wv"])
    wo = T(inputs["L0_attn_a_Wo"])
    wr = T(inputs["L0_ffn_W_r"])
    wfk = T(inputs["L0_ffn_W_k"])
    wfv = T(inputs["L0_ffn_W_v"])
    wqkv_raw = T(inputs["L0_attn_b_Wqkv"])
    perm = np.concatenate([np.concatenate([192 * h + off + np.arange(64)
                                           for h in range(16)])
                           for off in (0, 64, 128)])
    wqkv = np.ascontiguousarray(wqkv_raw[:, perm])
    wproj = T(inputs["L0_attn_b_Wproj"])
    wfT = T(inputs["L0_attn_a_Wf"])

    wfbdk = np.zeros((128, 512), dtype=np.float32)
    wfbdk[0:64, 0:256] = wfT
    wfbdk[64:128, 256:512] = wfT
    wfbdq = np.zeros((128, 512), dtype=np.float32)
    for j in range(4):
        wfbdq[0:64, 128 * j:128 * j + 64] = wfT[:, 64 * j:64 * j + 64]
        wfbdq[64:128, 128 * j + 64:128 * j + 128] = wfT[:, 64 * j:64 * j + 64]

    mu = np.zeros((128, 48), dtype=np.float32)
    mur = np.asarray(inputs["L0_ffn_mu_r"], dtype=np.float32).reshape(8, 128).T
    muk = np.asarray(inputs["L0_ffn_mu_k"], dtype=np.float32).reshape(8, 128).T
    bpr = np.asarray(inputs["L0_attn_b_bproj"], dtype=np.float32).reshape(8, 128).T
    mu[:, 0:8] = mur
    mu[:, 8:16] = 1.0 - mur
    mu[:, 16:24] = muk
    mu[:, 24:32] = 1.0 - muk
    mu[:, 32:40] = bpr

    ident = np.eye(128, dtype=ml_dtypes.bfloat16)
    pp, nn = np.meshgrid(np.arange(128), np.arange(128), indexing="ij")
    m01 = ((nn % 8) == (pp % 8)).astype(ml_dtypes.bfloat16)
    ones = np.ones((128, 4), dtype=np.float32)
    sel16 = np.zeros((16, 1024), dtype=np.float32)
    for c in range(8):
        sel16[2 * c, 128 * c:128 * c + 64] = 1.0
        sel16[2 * c + 1, 128 * c + 64:128 * c + 128] = 1.0

    shared = dict(wq=wq, wk=wk, wv=wv, wo=wo, wr=wr, wfk=wfk, wfv=wfv,
                  wqkv=wqkv, wproj=wproj, wfbdk=wfbdk, wfbdq=wfbdq, mu=mu,
                  ident=ident, m01=m01, onesin=ones, sel16=sel16)
    in_maps = []
    for core in range(NCORES):
        m = dict(shared)
        m["x"] = np.ascontiguousarray(xt[core * SEQ_PER_CORE:(core + 1) * SEQ_PER_CORE])
        in_maps.append(m)
    return in_maps


def kernel(**inputs):
    nc = build_program()
    in_maps = _host_inputs(inputs)
    res = run_bass_kernel_spmd(nc, in_maps, core_ids=list(range(NCORES)))
    outs = [res.results[c]["out"] for c in range(NCORES)]
    full = np.concatenate(outs, axis=0)
    return np.ascontiguousarray(full.transpose(0, 2, 1))


# version 05
# speedup vs baseline: 1.2437x; 1.2437x over previous
import contextlib
import numpy as np
import ml_dtypes

import concourse.bass as bass
import concourse.mybir as mybir
from concourse.bass_utils import run_bass_kernel_spmd

F32 = mybir.dt.float32
BF16 = mybir.dt.bfloat16
AF = mybir.ActivationFunctionType
ALU = mybir.AluOpType

B, S, D = 32, 2048, 1024
NCORES = 8
SEQ_PER_CORE = B // NCORES
NCH = D // 128
TT = 512
NTILES = S // TT
NSUB = TT // 128
H = 16
XL = 520
EPS = 1e-5

SUBLAYERS = (0, 1, 2, 3)
DBG = False
ATTNB_STAGE = 4


class Ctr:
    __slots__ = ("h", "n")

    def __init__(self, handle):
        self.h = handle
        self.n = 0


class Prog:
    def __init__(self, ctrs):
        self.streams = {e: [] for e in ("tensor", "vector", "scalar", "sync", "gpsimd")}
        self.c = ctrs
        self.last_wait = {}

    def op(self, eng, fn, inc=None, k=1):
        if inc is None:
            self.streams[eng].append(fn)
            return None
        ctr = self.c[inc]
        ctr.n += k
        h = ctr.h
        self.streams[eng].append(lambda e, fn=fn, h=h, k=k: fn(e).then_inc(h, k))
        return ctr.n

    def wait(self, eng, cname, val=None):
        ctr = self.c[cname]
        v = ctr.n if val is None else val
        if v <= 0:
            return
        if eng == cname:
            h = ctr.h
            self.streams[eng].append(lambda e, h=h, v=v: e.wait_ge(h, v))
            return
        key = (eng, cname)
        if self.last_wait.get(key, -1) >= v:
            return
        self.last_wait[key] = v
        h = ctr.h
        self.streams[eng].append(lambda e, h=h, v=v: e.wait_ge(h, v))


_CACHE = {}


def build_program():
    if "nc" in _CACHE:
        return _CACHE["nc"]
    nc = bass.Bass("TRN2", target_bir_lowering=False)
    es = contextlib.ExitStack()
    A = es.enter_context

    d_x = nc.dram_tensor("x", [SEQ_PER_CORE, D, S], F32, kind="ExternalInput").ap()
    d_out = nc.dram_tensor("out", [SEQ_PER_CORE, D, S], F32, kind="ExternalOutput").ap()
    dw = {}
    for nm in ("wq", "wk", "wv", "wo", "wr", "wfk", "wfv", "wproj"):
        dw[nm] = nc.dram_tensor(nm, [D, D], F32, kind="ExternalInput").ap()
    dw["wqkv"] = nc.dram_tensor("wqkv", [D, 3 * D], F32, kind="ExternalInput").ap()
    d_wfbdk = nc.dram_tensor("wfbdk", [128, 512], F32, kind="ExternalInput").ap()
    d_wfbdq = nc.dram_tensor("wfbdq", [128, 512], F32, kind="ExternalInput").ap()
    d_mu = nc.dram_tensor("mu", [128, 48], F32, kind="ExternalInput").ap()
    d_ident = nc.dram_tensor("ident", [128, 128], BF16, kind="ExternalInput").ap()
    d_m01 = nc.dram_tensor("m01", [128, 128], BF16, kind="ExternalInput").ap()
    d_ones = nc.dram_tensor("onesin", [128, 4], F32, kind="ExternalInput").ap()
    d_sel16 = nc.dram_tensor("sel16", [16, 1024], F32, kind="ExternalInput").ap()

    s_x = A(nc.sbuf_tensor("s_x", [128, NCH * S], F32))
    s_xlnb = A(nc.sbuf_tensor("s_xlnb", [128, NCH * XL], BF16))
    s_ar2 = A(nc.sbuf_tensor("s_ar2", [128, 16384], BF16))
    s_scr = A(nc.sbuf_tensor("s_scr", [128, 16384], BF16))
    s_w = [A(nc.sbuf_tensor(f"s_w{i}", [128, 8192], BF16)) for i in range(3)]
    s_wfbdk = A(nc.sbuf_tensor("s_wfbdk", [128, 512], BF16))
    s_wfbdq = A(nc.sbuf_tensor("s_wfbdq", [128, 512], BF16))
    s_mu = A(nc.sbuf_tensor("s_mu", [128, 48], F32))
    s_ident = A(nc.sbuf_tensor("s_ident", [128, 128], BF16))
    s_m01 = A(nc.sbuf_tensor("s_m01", [128, 128], BF16))
    s_onesb = A(nc.sbuf_tensor("s_onesb", [128, 4], BF16))
    s_onesrow = A(nc.sbuf_tensor("s_onesrow", [1, 128], F32))
    s_sel16 = A(nc.sbuf_tensor("s_sel16", [16, 1024], F32))
    s_xbc = A(nc.sbuf_tensor("s_xbc", [128, 2048], BF16))
    xbcF = s_xbc[:, :].bitcast(F32)
    s_lntmp = xbcF[:, 0:512]
    s_rowA = xbcF[0:1, 0:512]
    s_rowC = xbcF[0:1, 512:1024]
    s_rn16 = s_scr[0:16, 14848:15872].bitcast(F32)
    s_rn8 = A(nc.sbuf_tensor("s_rn8", [128, 8], F32))
    s_cst = A(nc.sbuf_tensor("s_cst", [128, 4], F32))
    s_expb = A(nc.sbuf_tensor("s_expb", [128, 1024], BF16))
    s_v8 = A(nc.sbuf_tensor("s_v8", [128, 8 * 65], BF16))
    s_outx = A(nc.sbuf_tensor("s_outx", [128, 8 * 64], BF16))
    s_low = A(nc.sbuf_tensor("s_low", [128, 4096], BF16))

    PA = A(nc.psum_tensor("p_a", [128, 512], F32))
    PB = A(nc.psum_tensor("p_b", [128, 512], F32))
    KVA = A(nc.psum_tensor("p_kva", [128, 1024], F32))
    KVB = A(nc.psum_tensor("p_kvb", [128, 1024], F32))
    KVC = A(nc.psum_tensor("p_kvc", [128, 1024], F32))

    ctrs = {n: Ctr(A(nc.semaphore("c_" + n))) for n in ("t", "v", "a", "s", "g")}
    block = A(nc.Block())
    p = Prog(ctrs)

    def kv_slice(sl):
        t = KVA if sl < 16 else KVB
        o = 64 * (sl % 16)
        return t[:, o:o + 64]

    oKT, oKF, oVT = 0, 4096, 8192
    oQF, oO, oBD, oNT = 0, 8192, 10240, 14336
    oXR, oXK, oKA = 0, 4096, 8192
    oQ, oK, oV = 0, 4096, 8192

    def gdma(dst, src):
        p.op("gpsimd", lambda e, d=dst, s_=src: e.dma_start(out=d, in_=s_), inc="g", k=16)

    def sdma(dst, src):
        p.op("sync", lambda e, d=dst, s_=src: e.dma_start(out=d, in_=s_), inc="s", k=16)

    def mm(out, lhsT, rhs, start, stop, inc=None, tile_position=None):
        p.op("tensor",
             lambda e, o=out, l=lhsT, r=rhs, st=start, sp=stop, tp=tile_position:
             e.matmul(o, l, r, start=st, stop=sp, tile_position=tp,
                      skip_group_check=True),
             inc=inc)

    def ttrans(out, in_, ident, inc=None, tile_position=None):
        p.op("tensor",
             lambda e, o=out, i=in_, d=ident, tp=tile_position:
             e.transpose(o, i, d, tile_position=tp),
             inc=inc)

    def vop(fn):
        p.op("vector", fn, inc="v")

    def aop(fn):
        p.op("scalar", fn, inc="a")

    def tmark():
        mm(PA[0:1, 0:1], s_onesb[0:1, 0:1], s_onesb[0:1, 0:1], True, True, inc="t")
        return ctrs["t"].n

    def barrier():
        tn = tmark()
        vn, an = ctrs["v"].n, ctrs["a"].n
        p.wait("tensor", "v", vn)
        p.wait("tensor", "a", an)
        p.wait("vector", "t", tn)
        p.wait("vector", "a", an)
        p.wait("scalar", "t", tn)
        p.wait("scalar", "v", vn)

    gdma(s_mu[:], d_mu[:])
    gdma(s_ident[:], d_ident[:])
    gdma(s_m01[:], d_m01[:])
    gdma(s_onesb[:], d_ones[:])
    gdma(s_sel16[:], d_sel16[:])
    gdma(s_wfbdk[:], d_wfbdk[:])
    gdma(s_wfbdq[:], d_wfbdq[:])
    g_const = ctrs["g"].n
    p.wait("vector", "g", g_const)
    vop(lambda e: e.memset(s_onesrow[:], 1.0))
    vop(lambda e: e.memset(s_cst[:, 0:1], -10.0))
    vop(lambda e: e.memset(s_cst[:, 1:2], EPS))
    vop(lambda e: e.memset(
        s_v8.rearrange("p (g c) -> p g c", g=8)[:, :, 64:65], 1.0))
    ONESB = s_onesb[:, 0:1]

    slot_loaded = [0, 0, 0]
    slot_freed = [0, 0, 0]

    def load_weight(slot, dram_ap):
        p.wait("gpsimd", "t", slot_freed[slot])
        src = dram_ap.rearrange("(c p) f -> p c f", p=128)
        dst = s_w[slot].rearrange("p (c f) -> p c f", c=8)
        gdma(dst, src)
        slot_loaded[slot] = ctrs["g"].n

    def free_slots():
        n = ctrs["t"].n
        slot_freed[0] = slot_freed[1] = slot_freed[2] = n

    def wtile(slot, c, m):
        return s_w[slot][:, 1024 * c + 128 * m: 1024 * c + 128 * m + 128]

    def emit_ln(tile):
        t0 = tile * TT
        marks = {}
        for c in range(NCH):
            pp = 1024 * (c % 2)
            xb = s_xbc[:, pp: pp + 512]
            xq = s_xbc[:, pp + 512: pp + 1024]
            if c >= 2:
                p.wait("vector", "t", marks[c - 2])
            xs = s_x[:, S * c + t0: S * c + t0 + TT]
            vop(lambda e, xb=xb, xs=xs: e.tensor_copy(out=xb, in_=xs))
            vop(lambda e, xq=xq, xb=xb: e.tensor_mul(xq, xb, xb))
            p.wait("tensor", "v")
            mm(PA[0:1, :], ONESB, xb, start=(c == 0), stop=(c == NCH - 1))
            mm(PB[0:1, :], ONESB, xq, start=(c == 0), stop=(c == NCH - 1), inc="t")
            marks[c] = ctrs["t"].n
        p.wait("vector", "t")
        rA, rC = s_rowA, s_rowC
        vop(lambda e: e.tensor_scalar_mul(rA, PA[0:1, :], 1.0 / D))
        vop(lambda e: e.tensor_mul(rC, rA, rA))
        vop(lambda e: e.scalar_tensor_tensor(out=PA[0:1, :], in0=PB[0:1, :],
                                             scalar=1.0 / D, in1=rC,
                                             op0=ALU.mult, op1=ALU.subtract))
        p.wait("scalar", "v")
        aop(lambda e: e.activation(out=rC, in_=PA[0:1, :], func=AF.Ln,
                                   bias=s_cst[0:1, 1:2]))
        aop(lambda e: e.activation(out=rC, in_=rC, func=AF.Exp, scale=-0.5))
        p.wait("vector", "a")
        vop(lambda e: e.tensor_mul(rA, rA, rC))
        p.wait("tensor", "v")
        p.wait("tensor", "a")
        mm(PA[:], s_onesrow[:], rC, start=True, stop=True)
        mm(PB[:], s_onesrow[:], rA, start=True, stop=True, inc="t")
        p.wait("vector", "t")
        for c in range(NCH):
            xs = s_x[:, S * c + t0: S * c + t0 + TT]
            dst = s_xlnb[:, XL * c + 1: XL * c + 1 + TT]
            vop(lambda e, xs=xs: e.tensor_mul(s_lntmp, xs, PA[:]))
            vop(lambda e, dst=dst: e.tensor_sub(dst, s_lntmp, PB[:]))

    def xlnb_c(c, lo, n):
        return s_xlnb[:, XL * c + 1 + lo: XL * c + 1 + lo + n]

    class BankMarks:
        def __init__(self, n=2):
            self.m = [None] * n

        def wait(self, bank):
            mk = self.m[bank]
            if mk is None:
                p.wait("tensor", "v")
                p.wait("tensor", "a")
            else:
                p.wait("tensor", "v", mk[0])
                p.wait("tensor", "a", mk[1])

        def set(self, bank):
            self.m[bank] = (ctrs["v"].n, ctrs["a"].n)

    def emit_proj(slot, rhs_fn, evac, w=512):
        bk = BankMarks()
        for m in range(NCH):
            ps = (PA if m % 2 == 0 else PB)[:, 0:w]
            bk.wait(m % 2)
            for c in range(NCH):
                mm(ps, wtile(slot, c, m), rhs_fn(c), start=(c == 0), stop=(c == NCH - 1),
                   inc=("t" if c == NCH - 1 else None))
            evac(m, ps)
            bk.set(m % 2)

    def elu_evac(ps, dst, w=512):
        p.wait("vector", "t")
        t1 = s_xbc[:, 0:w]
        t3 = s_xbc[:, 1024:1024 + w]
        vop(lambda e, t1=t1, ps=ps: e.tensor_scalar_min(t1, ps, 0.0))
        p.wait("scalar", "t")
        p.wait("scalar", "v")
        aop(lambda e, t1=t1: e.activation(out=t1, in_=t1, func=AF.Exp))
        aop(lambda e, t3=t3, ps=ps: e.activation(out=t3, in_=ps, func=AF.Relu))
        p.wait("vector", "a")
        vop(lambda e, dst=dst, t1=t1, t3=t3: e.tensor_add(dst, t1, t3))

    def emit_attn_a():
        QS = s_ar2
        load_weight(0, dw["wq"])
        load_weight(1, dw["wk"])
        load_weight(2, dw["wv"])
        for tile in range(NTILES):
            emit_ln(tile)
            p.wait("tensor", "g", slot_loaded[0])

            def ev_q(m, ps, tile=tile):
                p.wait("vector", "t")
                dst = QS[:, 2048 * m + TT * tile: 2048 * m + TT * tile + TT]
                vop(lambda e, dst=dst, ps=ps: e.tensor_copy(out=dst, in_=ps))
            emit_proj(0, lambda c: xlnb_c(c, 0, TT), ev_q)

            p.wait("tensor", "g", slot_loaded[1])

            def ev_k(m, ps):
                p.wait("vector", "t")
                dst = s_scr[:, oKT + 512 * m: oKT + 512 * m + TT]
                vop(lambda e, dst=dst, ps=ps: e.tensor_copy(out=dst, in_=ps))
            emit_proj(1, lambda c: xlnb_c(c, 0, TT), ev_k)
            p.wait("tensor", "g", slot_loaded[2])

            for sub in range(NSUB):
                lo = 128 * sub
                p.wait("tensor", "v")
                for c in range(NCH):
                    lhsT = xlnb_c(c, lo, 128)
                    mm(PA[:], lhsT, s_w[2][:, 1024 * c: 1024 * c + 512],
                       start=(c == 0), stop=(c == NCH - 1))
                    mm(PB[:], lhsT, s_w[2][:, 1024 * c + 512: 1024 * c + 1024],
                       start=(c == 0), stop=(c == NCH - 1),
                       inc=("t" if c == NCH - 1 else None))
                p.wait("vector", "t")
                for half, ps in ((0, PA[:]), (1, PB[:])):
                    dst = s_scr[:, oVT + 512 * half: oVT + 512 * half + 512]
                    vop(lambda e, dst=dst, ps=ps: e.tensor_copy(out=dst, in_=ps))
                bkk = BankMarks()
                for c in range(NCH):
                    ps = (PA if c % 2 == 0 else PB)[:]
                    bkk.wait(c % 2)
                    mm(ps, s_scr[:, oKT + 512 * c + lo: oKT + 512 * c + lo + 128],
                       s_wfbdk[:], start=True, stop=True, inc="t")
                    elu_evac(ps, s_scr[:, oKF + 512 * c: oKF + 512 * c + 512])
                    bkk.set(c % 2)
                p.wait("tensor", "v")
                first = (tile == 0 and sub == 0)
                last = (tile == NTILES - 1 and sub == NSUB - 1)
                for h in range(H):
                    for ph in range(2):
                        sl = 2 * h + ph
                        o = oKF + 512 * (h // 2) + 256 * (h % 2) + 128 * ph
                        mm(kv_slice(sl), s_scr[:, o:o + 128],
                           s_scr[:, oVT + 64 * h: oVT + 64 * h + 64],
                           start=first, stop=last)
                        mm(KVC[:, 2 * sl:2 * sl + 1], s_scr[:, o:o + 128],
                           ONESB, start=first, stop=last,
                           inc=("t" if sl == 31 else None))
            barrier()

        p.wait("vector", "t")
        for c in range(NCH):
            for j in range(4):
                idx = 4 * c + j
                bd = s_scr[:, oBD + 128 * idx: oBD + 128 * idx + 128]
                ntl = s_scr[:, oNT + 16 * idx: oNT + 16 * idx + 16]
                vop(lambda e, bd=bd: e.memset(bd, 0.0))
                vop(lambda e, ntl=ntl: e.memset(ntl, 0.0))
                for par in range(2):
                    h = 2 * c + par
                    sl = 2 * h + (j // 2)
                    src = kv_slice(sl)[64 * (j % 2):64 * (j % 2) + 64, :]
                    ks = KVC[64 * (j % 2):64 * (j % 2) + 64, 2 * sl:2 * sl + 1]
                    vop(lambda e, bd=bd, src=src, par=par:
                        e.tensor_copy(out=bd[64 * par:64 * par + 64,
                                             64 * par:64 * par + 64],
                                      in_=src))
                    vop(lambda e, ntl=ntl, ks=ks, par=par, h=h:
                        e.tensor_copy(out=ntl[64 * par:64 * par + 64, h:h + 1],
                                      in_=ks))
        barrier()

        free_slots()
        load_weight(0, dw["wo"])
        HT = 256
        for ht in range(S // HT):
            tb = HT * ht
            bkq = BankMarks()
            for c in range(NCH):
                for j in range(4):
                    idx = 4 * c + j
                    ps = (PA if idx % 2 == 0 else PB)[:, 0:HT]
                    bkq.wait(idx % 2)
                    mm(ps, s_wfbdq[:, 128 * j:128 * j + 128],
                       QS[:, 2048 * c + tb: 2048 * c + tb + HT],
                       start=True, stop=True, inc="t")
                    elu_evac(ps, s_scr[:, oQF + 256 * idx: oQF + 256 * idx + HT], w=HT)
                    bkq.set(idx % 2)
            p.wait("tensor", "v")
            for idx in range(32):
                mm(KVA[0:16, 0:HT], s_scr[:, oNT + 16 * idx: oNT + 16 * idx + 16],
                   s_scr[:, oQF + 256 * idx: oQF + 256 * idx + HT],
                   start=(idx == 0), stop=(idx == 31),
                   inc=("t" if idx == 31 else None))
            p.wait("vector", "t")
            rn = s_rn16[:, 0:HT]
            vop(lambda e, rn=rn: e.tensor_scalar_add(rn, KVA[0:16, 0:HT], 1e-6))
            vop(lambda e, rn=rn: e.reciprocal(out=rn, in_=rn))
            for c in range(NCH):
                p.wait("tensor", "v")
                mm(KVC[:, 0:HT], s_sel16[:, 128 * c:128 * c + 128], s_rn16[:, 0:HT],
                   start=True, stop=True)
                for j in range(4):
                    idx = 4 * c + j
                    mm(KVB[:, 0:HT], s_scr[:, oBD + 128 * idx: oBD + 128 * idx + 128],
                       s_scr[:, oQF + 256 * idx: oQF + 256 * idx + HT],
                       start=(j == 0), stop=(j == 3), inc=("t" if j == 3 else None))
                p.wait("vector", "t")
                dst = s_scr[:, oO + 256 * c: oO + 256 * c + HT]
                vop(lambda e: e.tensor_copy(out=s_lntmp[:, 0:HT], in_=KVC[:, 0:HT]))
                vop(lambda e, dst=dst: e.tensor_mul(dst, KVB[:, 0:HT],
                                                    s_lntmp[:, 0:HT]))
            p.wait("tensor", "g", slot_loaded[0])

            def ev_o(m, ps, tb=tb):
                p.wait("vector", "t")
                xs = s_x[:, S * m + tb: S * m + tb + HT]
                vop(lambda e, xs=xs, ps=ps: e.tensor_add(xs, xs, ps))
            emit_proj(0, lambda c: s_scr[:, oO + 256 * c: oO + 256 * c + HT],
                      ev_o, w=HT)
            barrier()
        free_slots()

    def emit_ffn():
        load_weight(0, dw["wr"])
        load_weight(1, dw["wfk"])
        load_weight(2, dw["wfv"])
        for tile in range(NTILES):
            emit_ln(tile)
            if tile == 0:
                for c in range(NCH):
                    vop(lambda e, c=c: e.memset(s_xlnb[:, XL * c:XL * c + 1], 0.0))
            for c in range(NCH):
                cur = s_xlnb[:, XL * c + 1: XL * c + 1 + TT]
                prev = s_xlnb[:, XL * c: XL * c + TT]
                for (oDST, mo, omo) in ((oXR, 0, 8), (oXK, 16, 24)):
                    t1 = s_xbc[:, 1024 * (c % 2): 1024 * (c % 2) + 512]
                    dst = s_scr[:, oDST + 512 * c: oDST + 512 * c + 512]
                    vop(lambda e, t1=t1, prev=prev, c=c, o=omo:
                        e.tensor_scalar_mul(t1, prev, s_mu[:, o + c:o + c + 1]))
                    vop(lambda e, dst=dst, cur=cur, t1=t1, c=c, o=mo:
                        e.scalar_tensor_tensor(out=dst, in0=cur,
                                               scalar=s_mu[:, o + c:o + c + 1],
                                               in1=t1, op0=ALU.mult, op1=ALU.add))
            for c in range(NCH):
                vop(lambda e, c=c: e.tensor_copy(
                    out=s_xlnb[:, XL * c:XL * c + 1],
                    in_=s_xlnb[:, XL * c + TT:XL * c + TT + 1]))
            p.wait("tensor", "g", slot_loaded[0])

            def ev_r(m, ps):
                p.wait("scalar", "t")
                dst = s_ar2[:, 1024 * m: 1024 * m + 1024].bitcast(F32)
                aop(lambda e, dst=dst, ps=ps: e.activation(out=dst, in_=ps,
                                                           func=AF.Sigmoid))
            emit_proj(0, lambda c: s_scr[:, oXR + 512 * c: oXR + 512 * c + 512], ev_r)
            p.wait("tensor", "g", slot_loaded[1])

            def ev_k(m, ps):
                p.wait("scalar", "t")
                p.wait("scalar", "v")
                t1 = s_xbc[:, 1024 * (m % 2): 1024 * (m % 2) + 512]
                dst = s_scr[:, oKA + 512 * m: oKA + 512 * m + 512]
                aop(lambda e, t1=t1, ps=ps: e.activation(out=t1, in_=ps, func=AF.Relu))
                p.wait("vector", "a")
                vop(lambda e, dst=dst, t1=t1: e.tensor_mul(dst, t1, t1))
            emit_proj(1, lambda c: s_scr[:, oXK + 512 * c: oXK + 512 * c + 512], ev_k)
            p.wait("tensor", "g", slot_loaded[2])
            t0 = tile * TT

            def ev_v(m, ps, t0=t0):
                p.wait("vector", "t")
                p.wait("vector", "a")
                r = s_ar2[:, 1024 * m: 1024 * m + 1024].bitcast(F32)
                xs = s_x[:, S * m + t0: S * m + t0 + TT]
                vop(lambda e, r=r, ps=ps: e.tensor_mul(s_lntmp, r, ps))
                vop(lambda e, xs=xs: e.tensor_add(xs, xs, s_lntmp))
            emit_proj(2, lambda c: s_scr[:, oKA + 512 * c: oKA + 512 * c + 512], ev_v)
            barrier()
        free_slots()

    def emit_attn_b(seqidx=0):
        OD = s_ar2
        load_weight(0, dw["wqkv"][:, 0:D])
        load_weight(1, dw["wqkv"][:, D:2 * D])
        load_weight(2, dw["wqkv"][:, 2 * D:3 * D])
        qlow = s_low[0:64, 0:2048]
        klow = s_low[0:64, 2048:4096]
        vlow = s_low[64:128, 0:2048]
        for tile in range(NTILES):
            emit_ln(tile)
            p.wait("vector", "s")
            for kslot, oDst in ((0, oQ), (1, oK), (2, oV)):
                p.wait("tensor", "g", slot_loaded[kslot])
                reg = s_scr[:, oDst:oDst + 4096].rearrange("p (g w) -> p g w", g=64)

                def ev(m, ps, reg=reg):
                    p.wait("vector", "t")
                    dst = reg[:, :, 8 * m:8 * m + 8]
                    psv = ps.rearrange("p (g t) -> p g t", g=64)
                    vop(lambda e, dst=dst, psv=psv: e.tensor_copy(out=dst, in_=psv))
                emit_proj(kslot, lambda c: xlnb_c(c, 0, TT), ev)
            for sub in range(NSUB):
                lo = 128 * sub
                p.wait("sync", "v")
                p.wait("sync", "t")
                for oSrc, lowt in ((oQ, qlow), (oK, klow), (oV, vlow)):
                    reg = s_scr[:, oSrc:oSrc + 4096].rearrange("p (g w) -> p g w", g=64)
                    se = reg[0:64, 16 * sub:16 * sub + 16, :]
                    so = reg[64:128, 16 * sub:16 * sub + 16, :]
                    lv = lowt.rearrange("p (g par w) -> p g par w", g=16, par=2)
                    sdma(lv[:, :, 0, :], se)
                    sdma(lv[:, :, 1, :], so)
                p.wait("tensor", "s")
                for gb in range(2):
                    p.wait("tensor", "a")
                    for g8 in range(8):
                        grp = 8 * gb + g8
                        mm(KVC[:, 128 * g8:128 * g8 + 128],
                           klow[:, 128 * grp:128 * grp + 128],
                           qlow[:, 128 * grp:128 * grp + 128],
                           start=True, stop=True, inc=("t" if g8 == 7 else None))
                    p.wait("scalar", "t")
                    p.wait("scalar", "v")
                    aop(lambda e: e.activation(out=s_expb[:], in_=KVC[:, 0:1024],
                                               func=AF.Exp, bias=s_cst[:, 0:1], scale=0.125))
                    p.wait("vector", "a")
                    for g8 in range(8):
                        ev_ = s_expb[:, 128 * g8:128 * g8 + 128]
                        vop(lambda e, ev_=ev_: e.tensor_tensor(
                            out=ev_, in0=ev_, in1=s_m01[:], op=ALU.mult))
                    if ATTNB_STAGE < 2:
                        continue
                    p.wait("tensor", "v")
                    for g8 in range(8):
                        grp = 8 * gb + g8
                        pslot = PA[:, 32 * g8:32 * g8 + 32].bitcast(BF16)
                        ttrans(pslot, vlow[:, 128 * grp:128 * grp + 128],
                               s_ident[64:128, 64:128],
                               tile_position=(64, 0),
                               inc=("t" if g8 == 7 else None))
                    p.wait("scalar", "t")
                    for g8 in range(8):
                        pslot = PA[:, 32 * g8:32 * g8 + 32].bitcast(BF16)
                        dst = s_v8[:, 65 * g8:65 * g8 + 64]
                        aop(lambda e, dst=dst, pslot=pslot: e.activation(
                            out=dst, in_=pslot, func=AF.Copy))
                    if ATTNB_STAGE < 3:
                        continue
                    p.wait("tensor", "a")
                    p.wait("tensor", "v")
                    for g8 in range(8):
                        mm(KVA[:, 128 * g8:128 * g8 + 65],
                           s_expb[:, 128 * g8:128 * g8 + 128],
                           s_v8[:, 65 * g8:65 * g8 + 65],
                           start=True, stop=True, inc=("t" if g8 == 7 else None))
                    p.wait("vector", "t")
                    kvv = KVA[:, 0:1024].rearrange("p (g c) -> p g c", g=8)
                    vop(lambda e, kvv=kvv: e.reciprocal(out=s_rn8[:],
                                                        in_=kvv[:, :, 64:65]))
                    p.wait("vector", "v")
                    for g8 in range(8):
                        ov = s_outx[:, 64 * g8:64 * g8 + 64]
                        kslc = KVA[:, 128 * g8:128 * g8 + 64]
                        vop(lambda e, ov=ov, kslc=kslc, g8=g8: e.tensor_scalar_mul(
                            ov, kslc, s_rn8[:, g8:g8 + 1]))
                    if ATTNB_STAGE < 4:
                        continue
                    p.wait("tensor", "v")
                    for g8 in range(8):
                        pslot = PA[:, 256 + 32 * g8:256 + 32 * g8 + 32].bitcast(BF16)
                        ine = s_outx[0:64, 64 * g8:64 * g8 + 64]
                        ino = s_outx[64:128, 64 * g8:64 * g8 + 64]
                        ttrans(pslot[0:64, :], ine, s_ident[0:64, 0:64])
                        ttrans(pslot[64:128, :], ino, s_ident[64:128, 64:128],
                               tile_position=(64, 64),
                               inc=("t" if g8 == 7 else None))
                    p.wait("scalar", "t")
                    gbase = (tile * TT + lo) // 8 + 8 * gb
                    dst = OD[:, 64 * gbase: 64 * gbase + 512]
                    psrc = PA[:, 256:512].bitcast(BF16)
                    aop(lambda e, dst=dst, psrc=psrc: e.activation(
                        out=dst, in_=psrc, func=AF.Copy))
                    if DBG and seqidx == 0 and tile == 0 and sub == 0 and gb == 0:
                        dbf = s_scr[:, 12288:14336]
                        dbF = dbf.bitcast(F32)
                        p.wait("vector", "t")
                        p.wait("vector", "a")
                        vop(lambda e: e.tensor_copy(out=dbF[:, 0:512],
                                                    in_=KVC[:, 0:512]))
                        vop(lambda e: e.tensor_copy(out=dbF[:, 512:640],
                                                    in_=s_expb[:, 0:128]))
                        vop(lambda e: e.tensor_copy(out=dbF[:, 640:705],
                                                    in_=s_v8[:, 0:65]))
                        vop(lambda e: e.tensor_copy(out=dbF[:, 705:769],
                                                    in_=s_outx[:, 0:64]))
                        vop(lambda e: e.tensor_copy(out=dbF[:, 769:834],
                                                    in_=KVA[:, 0:65]))
                        vop(lambda e: e.tensor_copy(out=dbF[:, 834:898],
                                                    in_=OD[:, 0:64]))
                        p.wait("sync", "v")
                        dv = d_out[0].rearrange("(c p) t -> p c t", p=128)
                        sdma(dv[:, 0, 0:1024], dbF[:, 0:1024])
            barrier()
        if ATTNB_STAGE < 4:
            free_slots()
            return
        free_slots()
        load_weight(0, dw["wproj"])
        p.wait("tensor", "g", slot_loaded[0])
        odv = OD.rearrange("p (G c) -> p G c", G=256)
        bkp = BankMarks()
        for tile in range(NTILES):
            t0 = tile * TT
            for m in range(NCH):
                idx = (tile * NCH + m) % 2
                PP = KVB if idx == 0 else KVA
                bkp.wait(idx)
                for hp in range(8):
                    le = s_w[0][0:64, 1024 * hp + 128 * m: 1024 * hp + 128 * m + 128]
                    lo_ = s_w[0][64:128, 1024 * hp + 128 * m: 1024 * hp + 128 * m + 128]
                    re = odv[0:64, 64 * tile:64 * tile + 64, 8 * hp:8 * hp + 8]
                    ro = odv[64:128, 64 * tile:64 * tile + 64, 8 * hp:8 * hp + 8]
                    mm(PP[:, 0:512], le, re, start=(hp == 0), stop=(hp == 7))
                    mm(PP[:, 512:1024], lo_, ro, start=(hp == 0), stop=(hp == 7),
                       inc=("t" if hp == 7 else None))
                p.wait("vector", "t")
                xs = s_x[:, S * m + t0: S * m + t0 + TT]
                bcol = s_mu[:, 32 + m:33 + m]
                vop(lambda e, xs=xs, bcol=bcol, PP=PP: e.scalar_tensor_tensor(
                    out=s_lntmp, in0=PP[:, 512:1024], scalar=bcol,
                    in1=xs, op0=ALU.add, op1=ALU.add))
                vop(lambda e, xs=xs, PP=PP: e.tensor_add(xs, s_lntmp, PP[:, 0:512]))
                bkp.set(idx)
        barrier()
        free_slots()

    for seq in range(1 if DBG else SEQ_PER_CORE):
        p.wait("sync", "s")
        p.wait("sync", "v")
        xv = d_x[seq].rearrange("(c p) t -> p c t", p=128)
        for c in range(NCH):
            sdma(s_x[:, S * c: S * c + S], xv[:, c, :])
        p.wait("tensor", "s")
        p.wait("vector", "s")
        if 0 in SUBLAYERS:
            emit_attn_a()
        if 1 in SUBLAYERS:
            emit_ffn()
        if 2 in SUBLAYERS:
            emit_attn_b(seq)
        if 3 in SUBLAYERS:
            emit_ffn()
        barrier()
        p.wait("sync", "v")
        if not DBG:
            ov = d_out[seq].rearrange("(c p) t -> p c t", p=128)
            for c in range(NCH):
                sdma(ov[:, c, :], s_x[:, S * c: S * c + S])
    p.wait("sync", "s")
    fin = ctrs["s"].n
    p.wait("gpsimd", "s", fin)

    @block.tensor
    def _(e):
        for fn in p.streams["tensor"]:
            fn(e)

    @block.vector
    def _(e):
        for fn in p.streams["vector"]:
            fn(e)

    @block.scalar
    def _(e):
        for fn in p.streams["scalar"]:
            fn(e)

    @block.sync
    def _(e):
        for fn in p.streams["sync"]:
            fn(e)

    @block.gpsimd
    def _(e):
        for fn in p.streams["gpsimd"]:
            fn(e)

    es.close()
    nc.finalize()
    _CACHE["nc"] = nc
    return nc



def _host_inputs(inputs):
    x = np.asarray(inputs["L0_attn_a_hidden_states"], dtype=np.float32)
    xt = np.ascontiguousarray(x.transpose(0, 2, 1))

    def T(a):
        return np.ascontiguousarray(np.asarray(a, dtype=np.float32).T)

    wq = T(inputs["L0_attn_a_Wq"])
    wk = T(inputs["L0_attn_a_Wk"])
    wv = T(inputs["L0_attn_a_Wv"])
    wo = T(inputs["L0_attn_a_Wo"])
    wr = T(inputs["L0_ffn_W_r"])
    wfk = T(inputs["L0_ffn_W_k"])
    wfv = T(inputs["L0_ffn_W_v"])
    wqkv_raw = T(inputs["L0_attn_b_Wqkv"])
    perm = np.concatenate([np.concatenate([192 * h + off + np.arange(64)
                                           for h in range(16)])
                           for off in (0, 64, 128)])
    wqkv = np.ascontiguousarray(wqkv_raw[:, perm])
    wproj = T(inputs["L0_attn_b_Wproj"])
    wfT = T(inputs["L0_attn_a_Wf"])

    wfbdk = np.zeros((128, 512), dtype=np.float32)
    wfbdk[0:64, 0:256] = wfT
    wfbdk[64:128, 256:512] = wfT
    wfbdq = np.zeros((128, 512), dtype=np.float32)
    for j in range(4):
        wfbdq[0:64, 128 * j:128 * j + 64] = wfT[:, 64 * j:64 * j + 64]
        wfbdq[64:128, 128 * j + 64:128 * j + 128] = wfT[:, 64 * j:64 * j + 64]

    mu = np.zeros((128, 48), dtype=np.float32)
    mur = np.asarray(inputs["L0_ffn_mu_r"], dtype=np.float32).reshape(8, 128).T
    muk = np.asarray(inputs["L0_ffn_mu_k"], dtype=np.float32).reshape(8, 128).T
    bpr = np.asarray(inputs["L0_attn_b_bproj"], dtype=np.float32).reshape(8, 128).T
    mu[:, 0:8] = mur
    mu[:, 8:16] = 1.0 - mur
    mu[:, 16:24] = muk
    mu[:, 24:32] = 1.0 - muk
    mu[:, 32:40] = bpr

    ident = np.eye(128, dtype=ml_dtypes.bfloat16)
    pp, nn = np.meshgrid(np.arange(128), np.arange(128), indexing="ij")
    m01 = ((nn % 8) == (pp % 8)).astype(ml_dtypes.bfloat16)
    ones = np.ones((128, 4), dtype=np.float32)
    sel16 = np.zeros((16, 1024), dtype=np.float32)
    for c in range(8):
        sel16[2 * c, 128 * c:128 * c + 64] = 1.0
        sel16[2 * c + 1, 128 * c + 64:128 * c + 128] = 1.0

    shared = dict(wq=wq, wk=wk, wv=wv, wo=wo, wr=wr, wfk=wfk, wfv=wfv,
                  wqkv=wqkv, wproj=wproj, wfbdk=wfbdk, wfbdq=wfbdq, mu=mu,
                  ident=ident, m01=m01, onesin=ones, sel16=sel16)
    in_maps = []
    for core in range(NCORES):
        m = dict(shared)
        m["x"] = np.ascontiguousarray(xt[core * SEQ_PER_CORE:(core + 1) * SEQ_PER_CORE])
        in_maps.append(m)
    return in_maps


def kernel(**inputs):
    nc = build_program()
    in_maps = _host_inputs(inputs)
    res = run_bass_kernel_spmd(nc, in_maps, core_ids=list(range(NCORES)))
    outs = [res.results[c]["out"] for c in range(NCORES)]
    full = np.concatenate(outs, axis=0)
    return np.ascontiguousarray(full.transpose(0, 2, 1))


# version 09
# speedup vs baseline: 1.2488x; 1.0041x over previous
import contextlib
import numpy as np
import ml_dtypes

import concourse.bass as bass
import concourse.mybir as mybir
from concourse.bass_utils import run_bass_kernel_spmd

F32 = mybir.dt.float32
BF16 = mybir.dt.bfloat16
AF = mybir.ActivationFunctionType
ALU = mybir.AluOpType

B, S, D = 32, 2048, 1024
NCORES = 8
SEQ_PER_CORE = B // NCORES
NCH = D // 128
TT = 512
NTILES = S // TT
NSUB = TT // 128
H = 16
XL = 520
EPS = 1e-5

SUBLAYERS = (0, 1, 2, 3)
DBG = False
ATTNB_STAGE = 4


class Ctr:
    __slots__ = ("h", "n")

    def __init__(self, handle):
        self.h = handle
        self.n = 0


class Prog:
    def __init__(self, ctrs):
        self.streams = {e: [] for e in ("tensor", "vector", "scalar", "sync", "gpsimd")}
        self.c = ctrs
        self.last_wait = {}

    def op(self, eng, fn, inc=None, k=1):
        if inc is None:
            self.streams[eng].append(fn)
            return None
        ctr = self.c[inc]
        ctr.n += k
        h = ctr.h
        self.streams[eng].append(lambda e, fn=fn, h=h, k=k: fn(e).then_inc(h, k))
        return ctr.n

    def wait(self, eng, cname, val=None):
        ctr = self.c[cname]
        v = ctr.n if val is None else val
        if v <= 0:
            return
        if eng == cname:
            h = ctr.h
            self.streams[eng].append(lambda e, h=h, v=v: e.wait_ge(h, v))
            return
        key = (eng, cname)
        if self.last_wait.get(key, -1) >= v:
            return
        self.last_wait[key] = v
        h = ctr.h
        self.streams[eng].append(lambda e, h=h, v=v: e.wait_ge(h, v))


_CACHE = {}


def build_program():
    if "nc" in _CACHE:
        return _CACHE["nc"]
    nc = bass.Bass("TRN2", target_bir_lowering=False)
    es = contextlib.ExitStack()
    A = es.enter_context

    d_x = nc.dram_tensor("x", [SEQ_PER_CORE, D, S], F32, kind="ExternalInput").ap()
    d_out = nc.dram_tensor("out", [SEQ_PER_CORE, D, S], F32, kind="ExternalOutput").ap()
    dw = {}
    for nm in ("wq", "wk", "wv", "wo", "wr", "wfk", "wfv", "wproj"):
        dw[nm] = nc.dram_tensor(nm, [D, D], BF16, kind="ExternalInput").ap()
    dw["wqkv"] = nc.dram_tensor("wqkv", [D, 3 * D], BF16, kind="ExternalInput").ap()
    d_wfbdk = nc.dram_tensor("wfbdk", [128, 512], BF16, kind="ExternalInput").ap()
    d_wfbdq = nc.dram_tensor("wfbdq", [128, 512], BF16, kind="ExternalInput").ap()
    d_mu = nc.dram_tensor("mu", [128, 48], F32, kind="ExternalInput").ap()
    d_ident = nc.dram_tensor("ident", [128, 128], BF16, kind="ExternalInput").ap()
    d_m01 = nc.dram_tensor("m01", [128, 128], BF16, kind="ExternalInput").ap()
    d_ones = nc.dram_tensor("onesin", [128, 4], F32, kind="ExternalInput").ap()
    d_sel16 = nc.dram_tensor("sel16", [16, 1024], F32, kind="ExternalInput").ap()

    s_x = A(nc.sbuf_tensor("s_x", [128, NCH * S], F32))
    s_xlnb = A(nc.sbuf_tensor("s_xlnb", [128, NCH * XL], BF16))
    s_ar2 = A(nc.sbuf_tensor("s_ar2", [128, 16384], BF16))
    s_scr = A(nc.sbuf_tensor("s_scr", [128, 16384], BF16))
    s_w = [A(nc.sbuf_tensor(f"s_w{i}", [128, 8192], BF16)) for i in range(3)]
    s_wfbdk = A(nc.sbuf_tensor("s_wfbdk", [128, 512], BF16))
    s_wfbdq = A(nc.sbuf_tensor("s_wfbdq", [128, 512], BF16))
    s_mu = A(nc.sbuf_tensor("s_mu", [128, 48], F32))
    s_ident = A(nc.sbuf_tensor("s_ident", [128, 128], BF16))
    s_m01 = A(nc.sbuf_tensor("s_m01", [128, 128], BF16))
    s_onesb = A(nc.sbuf_tensor("s_onesb", [128, 4], BF16))
    s_onesrow = A(nc.sbuf_tensor("s_onesrow", [1, 128], F32))
    s_sel16 = A(nc.sbuf_tensor("s_sel16", [16, 1024], F32))
    s_xbc = A(nc.sbuf_tensor("s_xbc", [128, 2048], BF16))
    xbcF = s_xbc[:, :].bitcast(F32)
    s_lntmp = xbcF[:, 0:512]
    s_rowA = xbcF[0:1, 0:512]
    s_rowC = xbcF[0:1, 512:1024]
    s_rn16 = s_scr[0:16, 14848:15872].bitcast(F32)
    s_rn8 = A(nc.sbuf_tensor("s_rn8", [128, 8], F32))
    s_cst = A(nc.sbuf_tensor("s_cst", [128, 4], F32))
    s_expb = A(nc.sbuf_tensor("s_expb", [128, 1024], BF16))
    s_v8 = A(nc.sbuf_tensor("s_v8", [128, 8 * 65], BF16))
    s_outx = A(nc.sbuf_tensor("s_outx", [128, 8 * 64], BF16))
    s_low = A(nc.sbuf_tensor("s_low", [128, 4096], BF16))

    PA = A(nc.psum_tensor("p_a", [128, 512], F32))
    PB = A(nc.psum_tensor("p_b", [128, 512], F32))
    KVA = A(nc.psum_tensor("p_kva", [128, 1024], F32))
    KVB = A(nc.psum_tensor("p_kvb", [128, 1024], F32))
    KVC = A(nc.psum_tensor("p_kvc", [128, 1024], F32))

    ctrs = {n: Ctr(A(nc.semaphore("c_" + n))) for n in ("t", "v", "a", "s", "g")}
    block = A(nc.Block())
    p = Prog(ctrs)

    def kv_slice(sl):
        t = KVA if sl < 16 else KVB
        o = 64 * (sl % 16)
        return t[:, o:o + 64]

    oKT, oKF, oVT = 0, 4096, 8192
    oQF, oO, oBD, oNT = 0, 8192, 10240, 14336
    oXR, oXK, oKA = 0, 4096, 8192
    oQ, oK, oV = 0, 4096, 8192

    def gdma(dst, src):
        p.op("gpsimd", lambda e, d=dst, s_=src: e.dma_start(out=d, in_=s_), inc="g", k=16)

    def sdma(dst, src):
        p.op("sync", lambda e, d=dst, s_=src: e.dma_start(out=d, in_=s_), inc="s", k=16)

    def mm(out, lhsT, rhs, start, stop, inc=None, tile_position=None):
        p.op("tensor",
             lambda e, o=out, l=lhsT, r=rhs, st=start, sp=stop, tp=tile_position:
             e.matmul(o, l, r, start=st, stop=sp, tile_position=tp,
                      skip_group_check=True),
             inc=inc)

    def ttrans(out, in_, ident, inc=None, tile_position=None):
        p.op("tensor",
             lambda e, o=out, i=in_, d=ident, tp=tile_position:
             e.transpose(o, i, d, tile_position=tp),
             inc=inc)

    def vop(fn):
        p.op("vector", fn, inc="v")

    def aop(fn):
        p.op("scalar", fn, inc="a")

    def tmark():
        mm(PA[0:1, 0:1], s_onesb[0:1, 0:1], s_onesb[0:1, 0:1], True, True, inc="t")
        return ctrs["t"].n

    def barrier():
        tn = tmark()
        vn, an = ctrs["v"].n, ctrs["a"].n
        p.wait("tensor", "v", vn)
        p.wait("tensor", "a", an)
        p.wait("vector", "t", tn)
        p.wait("vector", "a", an)
        p.wait("scalar", "t", tn)
        p.wait("scalar", "v", vn)

    gdma(s_mu[:], d_mu[:])
    gdma(s_ident[:], d_ident[:])
    gdma(s_m01[:], d_m01[:])
    gdma(s_onesb[:], d_ones[:])
    gdma(s_sel16[:], d_sel16[:])
    gdma(s_wfbdk[:], d_wfbdk[:])
    gdma(s_wfbdq[:], d_wfbdq[:])
    g_const = ctrs["g"].n
    p.wait("vector", "g", g_const)
    vop(lambda e: e.memset(s_onesrow[:], 1.0))
    vop(lambda e: e.memset(s_cst[:, 0:1], -10.0))
    vop(lambda e: e.memset(s_cst[:, 1:2], EPS))
    vop(lambda e: e.memset(
        s_v8.rearrange("p (g c) -> p g c", g=8)[:, :, 64:65], 1.0))
    ONESB = s_onesb[:, 0:1]

    slot_loaded = [0, 0, 0]
    slot_freed = [0, 0, 0]

    def load_weight(slot, dram_ap):
        p.wait("gpsimd", "t", slot_freed[slot])
        src = dram_ap.rearrange("(c p) f -> p c f", p=128)
        dst = s_w[slot].rearrange("p (c f) -> p c f", c=8)
        gdma(dst, src)
        slot_loaded[slot] = ctrs["g"].n

    def free_slots():
        n = ctrs["t"].n
        slot_freed[0] = slot_freed[1] = slot_freed[2] = n

    def wtile(slot, c, m):
        return s_w[slot][:, 1024 * c + 128 * m: 1024 * c + 128 * m + 128]

    def emit_ln(tile):
        t0 = tile * TT
        marks = {}
        for c in range(NCH):
            pp = 1024 * (c % 2)
            xb = s_xbc[:, pp: pp + 512]
            xq = s_xbc[:, pp + 512: pp + 1024]
            if c >= 2:
                p.wait("vector", "t", marks[c - 2])
            xs = s_x[:, S * c + t0: S * c + t0 + TT]
            vop(lambda e, xb=xb, xs=xs: e.tensor_copy(out=xb, in_=xs))
            vop(lambda e, xq=xq, xb=xb: e.tensor_mul(xq, xb, xb))
            p.wait("tensor", "v")
            mm(PA[0:1, :], ONESB, xb, start=(c == 0), stop=(c == NCH - 1))
            mm(PB[0:1, :], ONESB, xq, start=(c == 0), stop=(c == NCH - 1), inc="t")
            marks[c] = ctrs["t"].n
        p.wait("vector", "t")
        rA, rC = s_rowA, s_rowC
        vop(lambda e: e.tensor_scalar_mul(rA, PA[0:1, :], 1.0 / D))
        vop(lambda e: e.tensor_mul(rC, rA, rA))
        vop(lambda e: e.scalar_tensor_tensor(out=PA[0:1, :], in0=PB[0:1, :],
                                             scalar=1.0 / D, in1=rC,
                                             op0=ALU.mult, op1=ALU.subtract))
        p.wait("scalar", "v")
        aop(lambda e: e.activation(out=rC, in_=PA[0:1, :], func=AF.Ln,
                                   bias=s_cst[0:1, 1:2]))
        aop(lambda e: e.activation(out=rC, in_=rC, func=AF.Exp, scale=-0.5))
        p.wait("vector", "a")
        vop(lambda e: e.tensor_mul(rA, rA, rC))
        p.wait("tensor", "v")
        p.wait("tensor", "a")
        mm(PA[:], s_onesrow[:], rC, start=True, stop=True)
        mm(PB[:], s_onesrow[:], rA, start=True, stop=True, inc="t")
        p.wait("vector", "t")
        for c in range(NCH):
            xs = s_x[:, S * c + t0: S * c + t0 + TT]
            dst = s_xlnb[:, XL * c + 1: XL * c + 1 + TT]
            vop(lambda e, xs=xs: e.tensor_mul(s_lntmp, xs, PA[:]))
            vop(lambda e, dst=dst: e.tensor_sub(dst, s_lntmp, PB[:]))

    def xlnb_c(c, lo, n):
        return s_xlnb[:, XL * c + 1 + lo: XL * c + 1 + lo + n]

    class BankMarks:
        def __init__(self, n=2):
            self.m = [None] * n

        def wait(self, bank):
            mk = self.m[bank]
            if mk is None:
                p.wait("tensor", "v")
                p.wait("tensor", "a")
            else:
                p.wait("tensor", "v", mk[0])
                p.wait("tensor", "a", mk[1])

        def set(self, bank):
            self.m[bank] = (ctrs["v"].n, ctrs["a"].n)

    def emit_proj(slot, rhs_fn, evac, w=512):
        bk = BankMarks()
        for m in range(NCH):
            ps = (PA if m % 2 == 0 else PB)[:, 0:w]
            bk.wait(m % 2)
            for c in range(NCH):
                mm(ps, wtile(slot, c, m), rhs_fn(c), start=(c == 0), stop=(c == NCH - 1),
                   inc=("t" if c == NCH - 1 else None))
            evac(m, ps)
            bk.set(m % 2)

    def elu_evac(ps, dst, w=512):
        p.wait("vector", "t")
        t1 = s_xbc[:, 0:w]
        t3 = s_xbc[:, 1024:1024 + w]
        vop(lambda e, t1=t1, ps=ps: e.tensor_scalar_min(t1, ps, 0.0))
        p.wait("scalar", "t")
        p.wait("scalar", "v")
        aop(lambda e, t1=t1: e.activation(out=t1, in_=t1, func=AF.Exp))
        aop(lambda e, t3=t3, ps=ps: e.activation(out=t3, in_=ps, func=AF.Relu))
        p.wait("vector", "a")
        vop(lambda e, dst=dst, t1=t1, t3=t3: e.tensor_add(dst, t1, t3))

    def emit_attn_a():
        QS = s_ar2
        load_weight(0, dw["wq"])
        load_weight(1, dw["wk"])
        load_weight(2, dw["wv"])
        for tile in range(NTILES):
            emit_ln(tile)
            p.wait("tensor", "g", slot_loaded[0])

            def ev_q(m, ps, tile=tile):
                p.wait("vector", "t")
                dst = QS[:, 2048 * m + TT * tile: 2048 * m + TT * tile + TT]
                vop(lambda e, dst=dst, ps=ps: e.tensor_copy(out=dst, in_=ps))
            emit_proj(0, lambda c: xlnb_c(c, 0, TT), ev_q)

            p.wait("tensor", "g", slot_loaded[1])

            def ev_k(m, ps):
                p.wait("vector", "t")
                dst = s_scr[:, oKT + 512 * m: oKT + 512 * m + TT]
                vop(lambda e, dst=dst, ps=ps: e.tensor_copy(out=dst, in_=ps))
            emit_proj(1, lambda c: xlnb_c(c, 0, TT), ev_k)
            p.wait("tensor", "g", slot_loaded[2])

            for sub in range(NSUB):
                lo = 128 * sub
                p.wait("tensor", "v")
                for c in range(NCH):
                    lhsT = xlnb_c(c, lo, 128)
                    mm(PA[:], lhsT, s_w[2][:, 1024 * c: 1024 * c + 512],
                       start=(c == 0), stop=(c == NCH - 1))
                    mm(PB[:], lhsT, s_w[2][:, 1024 * c + 512: 1024 * c + 1024],
                       start=(c == 0), stop=(c == NCH - 1),
                       inc=("t" if c == NCH - 1 else None))
                p.wait("vector", "t")
                for half, ps in ((0, PA[:]), (1, PB[:])):
                    dst = s_scr[:, oVT + 512 * half: oVT + 512 * half + 512]
                    vop(lambda e, dst=dst, ps=ps: e.tensor_copy(out=dst, in_=ps))
                bkk = BankMarks()
                for c in range(NCH):
                    ps = (PA if c % 2 == 0 else PB)[:]
                    bkk.wait(c % 2)
                    mm(ps, s_scr[:, oKT + 512 * c + lo: oKT + 512 * c + lo + 128],
                       s_wfbdk[:], start=True, stop=True, inc="t")
                    elu_evac(ps, s_scr[:, oKF + 512 * c: oKF + 512 * c + 512])
                    bkk.set(c % 2)
                p.wait("tensor", "v")
                first = (tile == 0 and sub == 0)
                last = (tile == NTILES - 1 and sub == NSUB - 1)
                for h in range(H):
                    for ph in range(2):
                        sl = 2 * h + ph
                        o = oKF + 512 * (h // 2) + 256 * (h % 2) + 128 * ph
                        mm(kv_slice(sl), s_scr[:, o:o + 128],
                           s_scr[:, oVT + 64 * h: oVT + 64 * h + 64],
                           start=first, stop=last)
                        mm(KVC[:, 2 * sl:2 * sl + 1], s_scr[:, o:o + 128],
                           ONESB, start=first, stop=last,
                           inc=("t" if sl == 31 else None))
            barrier()

        p.wait("vector", "t")
        for c in range(NCH):
            for j in range(4):
                idx = 4 * c + j
                bd = s_scr[:, oBD + 128 * idx: oBD + 128 * idx + 128]
                ntl = s_scr[:, oNT + 16 * idx: oNT + 16 * idx + 16]
                vop(lambda e, bd=bd: e.memset(bd, 0.0))
                vop(lambda e, ntl=ntl: e.memset(ntl, 0.0))
                for par in range(2):
                    h = 2 * c + par
                    sl = 2 * h + (j // 2)
                    src = kv_slice(sl)[64 * (j % 2):64 * (j % 2) + 64, :]
                    ks = KVC[64 * (j % 2):64 * (j % 2) + 64, 2 * sl:2 * sl + 1]
                    vop(lambda e, bd=bd, src=src, par=par:
                        e.tensor_copy(out=bd[64 * par:64 * par + 64,
                                             64 * par:64 * par + 64],
                                      in_=src))
                    vop(lambda e, ntl=ntl, ks=ks, par=par, h=h:
                        e.tensor_copy(out=ntl[64 * par:64 * par + 64, h:h + 1],
                                      in_=ks))
        barrier()

        free_slots()
        load_weight(0, dw["wo"])
        HT = 256
        for ht in range(S // HT):
            tb = HT * ht
            bkq = BankMarks()
            for c in range(NCH):
                for j in range(4):
                    idx = 4 * c + j
                    ps = (PA if idx % 2 == 0 else PB)[:, 0:HT]
                    bkq.wait(idx % 2)
                    mm(ps, s_wfbdq[:, 128 * j:128 * j + 128],
                       QS[:, 2048 * c + tb: 2048 * c + tb + HT],
                       start=True, stop=True, inc="t")
                    elu_evac(ps, s_scr[:, oQF + 256 * idx: oQF + 256 * idx + HT], w=HT)
                    bkq.set(idx % 2)
            p.wait("tensor", "v")
            for idx in range(32):
                mm(KVA[0:16, 0:HT], s_scr[:, oNT + 16 * idx: oNT + 16 * idx + 16],
                   s_scr[:, oQF + 256 * idx: oQF + 256 * idx + HT],
                   start=(idx == 0), stop=(idx == 31),
                   inc=("t" if idx == 31 else None))
            p.wait("vector", "t")
            rn = s_rn16[:, 0:HT]
            vop(lambda e, rn=rn: e.tensor_scalar_add(rn, KVA[0:16, 0:HT], 1e-6))
            vop(lambda e, rn=rn: e.reciprocal(out=rn, in_=rn))
            for c in range(NCH):
                p.wait("tensor", "v")
                mm(KVC[:, 0:HT], s_sel16[:, 128 * c:128 * c + 128], s_rn16[:, 0:HT],
                   start=True, stop=True)
                for j in range(4):
                    idx = 4 * c + j
                    mm(KVB[:, 0:HT], s_scr[:, oBD + 128 * idx: oBD + 128 * idx + 128],
                       s_scr[:, oQF + 256 * idx: oQF + 256 * idx + HT],
                       start=(j == 0), stop=(j == 3), inc=("t" if j == 3 else None))
                p.wait("vector", "t")
                dst = s_scr[:, oO + 256 * c: oO + 256 * c + HT]
                vop(lambda e: e.tensor_copy(out=s_lntmp[:, 0:HT], in_=KVC[:, 0:HT]))
                vop(lambda e, dst=dst: e.tensor_mul(dst, KVB[:, 0:HT],
                                                    s_lntmp[:, 0:HT]))
            p.wait("tensor", "g", slot_loaded[0])

            def ev_o(m, ps, tb=tb):
                p.wait("vector", "t")
                xs = s_x[:, S * m + tb: S * m + tb + HT]
                vop(lambda e, xs=xs, ps=ps: e.tensor_add(xs, xs, ps))
            emit_proj(0, lambda c: s_scr[:, oO + 256 * c: oO + 256 * c + HT],
                      ev_o, w=HT)
            barrier()
        free_slots()

    def emit_ffn(io_seq=None):
        load_weight(0, dw["wr"])
        load_weight(1, dw["wfk"])
        load_weight(2, dw["wfv"])
        for tile in range(NTILES):
            emit_ln(tile)
            if tile == 0:
                for c in range(NCH):
                    vop(lambda e, c=c: e.memset(s_xlnb[:, XL * c:XL * c + 1], 0.0))
            for c in range(NCH):
                cur = s_xlnb[:, XL * c + 1: XL * c + 1 + TT]
                prev = s_xlnb[:, XL * c: XL * c + TT]
                for (oDST, mo, omo) in ((oXR, 0, 8), (oXK, 16, 24)):
                    t1 = s_xbc[:, 1024 * (c % 2): 1024 * (c % 2) + 512]
                    dst = s_scr[:, oDST + 512 * c: oDST + 512 * c + 512]
                    vop(lambda e, t1=t1, prev=prev, c=c, o=omo:
                        e.tensor_scalar_mul(t1, prev, s_mu[:, o + c:o + c + 1]))
                    vop(lambda e, dst=dst, cur=cur, t1=t1, c=c, o=mo:
                        e.scalar_tensor_tensor(out=dst, in0=cur,
                                               scalar=s_mu[:, o + c:o + c + 1],
                                               in1=t1, op0=ALU.mult, op1=ALU.add))
            for c in range(NCH):
                vop(lambda e, c=c: e.tensor_copy(
                    out=s_xlnb[:, XL * c:XL * c + 1],
                    in_=s_xlnb[:, XL * c + TT:XL * c + TT + 1]))
            p.wait("tensor", "g", slot_loaded[0])

            def ev_r(m, ps):
                p.wait("scalar", "t")
                dst = s_ar2[:, 1024 * m: 1024 * m + 1024].bitcast(F32)
                aop(lambda e, dst=dst, ps=ps: e.activation(out=dst, in_=ps,
                                                           func=AF.Sigmoid))
            emit_proj(0, lambda c: s_scr[:, oXR + 512 * c: oXR + 512 * c + 512], ev_r)
            p.wait("tensor", "g", slot_loaded[1])

            def ev_k(m, ps):
                p.wait("scalar", "t")
                p.wait("scalar", "v")
                t1 = s_xbc[:, 1024 * (m % 2): 1024 * (m % 2) + 512]
                dst = s_scr[:, oKA + 512 * m: oKA + 512 * m + 512]
                aop(lambda e, t1=t1, ps=ps: e.activation(out=t1, in_=ps, func=AF.Relu))
                p.wait("vector", "a")
                vop(lambda e, dst=dst, t1=t1: e.tensor_mul(dst, t1, t1))
            emit_proj(1, lambda c: s_scr[:, oXK + 512 * c: oXK + 512 * c + 512], ev_k)
            p.wait("tensor", "g", slot_loaded[2])
            t0 = tile * TT

            def ev_v(m, ps, t0=t0, tile=tile):
                p.wait("vector", "t")
                p.wait("vector", "a")
                r = s_ar2[:, 1024 * m: 1024 * m + 1024].bitcast(F32)
                xs = s_x[:, S * m + t0: S * m + t0 + TT]
                vop(lambda e, r=r, ps=ps: e.tensor_mul(s_lntmp, r, ps))
                vop(lambda e, xs=xs: e.tensor_add(xs, xs, s_lntmp))
                if io_seq is not None and tile == NTILES - 1:
                    p.wait("sync", "v")
                    ov = d_out[io_seq].rearrange("(c p) t -> p c t", p=128)
                    sdma(ov[:, m, :], s_x[:, S * m: S * m + S])
                    if io_seq + 1 < SEQ_PER_CORE:
                        p.wait("sync", "s")
                        xv = d_x[io_seq + 1].rearrange("(c p) t -> p c t", p=128)
                        sdma(s_x[:, S * m: S * m + S], xv[:, m, :])
            emit_proj(2, lambda c: s_scr[:, oKA + 512 * c: oKA + 512 * c + 512], ev_v)
            barrier()
        free_slots()

    def emit_attn_b(seqidx=0):
        OD = s_ar2
        load_weight(0, dw["wqkv"][:, 0:D])
        load_weight(1, dw["wqkv"][:, D:2 * D])
        load_weight(2, dw["wqkv"][:, 2 * D:3 * D])
        qlow = s_low[0:64, 0:2048]
        klow = s_low[0:64, 2048:4096]
        vlow = s_low[64:128, 0:2048]
        for tile in range(NTILES):
            emit_ln(tile)
            p.wait("vector", "s")
            for kslot, oDst in ((0, oQ), (1, oK), (2, oV)):
                p.wait("tensor", "g", slot_loaded[kslot])
                reg = s_scr[:, oDst:oDst + 4096].rearrange("p (g w) -> p g w", g=64)

                def ev(m, ps, reg=reg):
                    p.wait("vector", "t")
                    dst = reg[:, :, 8 * m:8 * m + 8]
                    psv = ps.rearrange("p (g t) -> p g t", g=64)
                    vop(lambda e, dst=dst, psv=psv: e.tensor_copy(out=dst, in_=psv))
                emit_proj(kslot, lambda c: xlnb_c(c, 0, TT), ev)
            for sub in range(NSUB):
                lo = 128 * sub
                p.wait("sync", "v")
                p.wait("sync", "t")
                for oSrc, lowt in ((oQ, qlow), (oK, klow), (oV, vlow)):
                    reg = s_scr[:, oSrc:oSrc + 4096].rearrange("p (g w) -> p g w", g=64)
                    se = reg[0:64, 16 * sub:16 * sub + 16, :]
                    so = reg[64:128, 16 * sub:16 * sub + 16, :]
                    lv = lowt.rearrange("p (g par w) -> p g par w", g=16, par=2)
                    sdma(lv[:, :, 0, :], se)
                    sdma(lv[:, :, 1, :], so)
                p.wait("tensor", "s")
                for gb in range(2):
                    p.wait("tensor", "a")
                    for g8 in range(8):
                        grp = 8 * gb + g8
                        mm(KVC[:, 128 * g8:128 * g8 + 128],
                           klow[:, 128 * grp:128 * grp + 128],
                           qlow[:, 128 * grp:128 * grp + 128],
                           start=True, stop=True, inc=("t" if g8 == 7 else None))
                    p.wait("scalar", "t")
                    p.wait("scalar", "v")
                    aop(lambda e: e.activation(out=s_expb[:], in_=KVC[:, 0:1024],
                                               func=AF.Exp, bias=s_cst[:, 0:1], scale=0.125))
                    p.wait("vector", "a")
                    for g8 in range(8):
                        ev_ = s_expb[:, 128 * g8:128 * g8 + 128]
                        vop(lambda e, ev_=ev_: e.tensor_tensor(
                            out=ev_, in0=ev_, in1=s_m01[:], op=ALU.mult))
                    if ATTNB_STAGE < 2:
                        continue
                    p.wait("tensor", "v")
                    for g8 in range(8):
                        grp = 8 * gb + g8
                        pslot = PA[:, 32 * g8:32 * g8 + 32].bitcast(BF16)
                        ttrans(pslot, vlow[:, 128 * grp:128 * grp + 128],
                               s_ident[64:128, 64:128],
                               tile_position=(64, 0),
                               inc=("t" if g8 == 7 else None))
                    p.wait("scalar", "t")
                    for g8 in range(8):
                        pslot = PA[:, 32 * g8:32 * g8 + 32].bitcast(BF16)
                        dst = s_v8[:, 65 * g8:65 * g8 + 64]
                        aop(lambda e, dst=dst, pslot=pslot: e.activation(
                            out=dst, in_=pslot, func=AF.Copy))
                    if ATTNB_STAGE < 3:
                        continue
                    p.wait("tensor", "a")
                    p.wait("tensor", "v")
                    for g8 in range(8):
                        mm(KVA[:, 128 * g8:128 * g8 + 65],
                           s_expb[:, 128 * g8:128 * g8 + 128],
                           s_v8[:, 65 * g8:65 * g8 + 65],
                           start=True, stop=True, inc=("t" if g8 == 7 else None))
                    p.wait("vector", "t")
                    kvv = KVA[:, 0:1024].rearrange("p (g c) -> p g c", g=8)
                    vop(lambda e, kvv=kvv: e.reciprocal(out=s_rn8[:],
                                                        in_=kvv[:, :, 64:65]))
                    p.wait("vector", "v")
                    for g8 in range(8):
                        ov = s_outx[:, 64 * g8:64 * g8 + 64]
                        kslc = KVA[:, 128 * g8:128 * g8 + 64]
                        vop(lambda e, ov=ov, kslc=kslc, g8=g8: e.tensor_scalar_mul(
                            ov, kslc, s_rn8[:, g8:g8 + 1]))
                    if ATTNB_STAGE < 4:
                        continue
                    p.wait("tensor", "v")
                    for g8 in range(8):
                        pslot = PA[:, 256 + 32 * g8:256 + 32 * g8 + 32].bitcast(BF16)
                        ine = s_outx[0:64, 64 * g8:64 * g8 + 64]
                        ino = s_outx[64:128, 64 * g8:64 * g8 + 64]
                        ttrans(pslot[0:64, :], ine, s_ident[0:64, 0:64])
                        ttrans(pslot[64:128, :], ino, s_ident[64:128, 64:128],
                               tile_position=(64, 64),
                               inc=("t" if g8 == 7 else None))
                    p.wait("scalar", "t")
                    gbase = (tile * TT + lo) // 8 + 8 * gb
                    dst = OD[:, 64 * gbase: 64 * gbase + 512]
                    psrc = PA[:, 256:512].bitcast(BF16)
                    aop(lambda e, dst=dst, psrc=psrc: e.activation(
                        out=dst, in_=psrc, func=AF.Copy))
                    if DBG and seqidx == 0 and tile == 0 and sub == 0 and gb == 0:
                        dbf = s_scr[:, 12288:14336]
                        dbF = dbf.bitcast(F32)
                        p.wait("vector", "t")
                        p.wait("vector", "a")
                        vop(lambda e: e.tensor_copy(out=dbF[:, 0:512],
                                                    in_=KVC[:, 0:512]))
                        vop(lambda e: e.tensor_copy(out=dbF[:, 512:640],
                                                    in_=s_expb[:, 0:128]))
                        vop(lambda e: e.tensor_copy(out=dbF[:, 640:705],
                                                    in_=s_v8[:, 0:65]))
                        vop(lambda e: e.tensor_copy(out=dbF[:, 705:769],
                                                    in_=s_outx[:, 0:64]))
                        vop(lambda e: e.tensor_copy(out=dbF[:, 769:834],
                                                    in_=KVA[:, 0:65]))
                        vop(lambda e: e.tensor_copy(out=dbF[:, 834:898],
                                                    in_=OD[:, 0:64]))
                        p.wait("sync", "v")
                        dv = d_out[0].rearrange("(c p) t -> p c t", p=128)
                        sdma(dv[:, 0, 0:1024], dbF[:, 0:1024])
            barrier()
        if ATTNB_STAGE < 4:
            free_slots()
            return
        free_slots()
        load_weight(0, dw["wproj"])
        p.wait("tensor", "g", slot_loaded[0])
        odv = OD.rearrange("p (G c) -> p G c", G=256)
        bkp = BankMarks()
        for tile in range(NTILES):
            t0 = tile * TT
            for m in range(NCH):
                idx = (tile * NCH + m) % 2
                PP = KVB if idx == 0 else KVA
                bkp.wait(idx)
                for hp in range(8):
                    le = s_w[0][0:64, 1024 * hp + 128 * m: 1024 * hp + 128 * m + 128]
                    lo_ = s_w[0][64:128, 1024 * hp + 128 * m: 1024 * hp + 128 * m + 128]
                    re = odv[0:64, 64 * tile:64 * tile + 64, 8 * hp:8 * hp + 8]
                    ro = odv[64:128, 64 * tile:64 * tile + 64, 8 * hp:8 * hp + 8]
                    mm(PP[:, 0:512], le, re, start=(hp == 0), stop=(hp == 7))
                    mm(PP[:, 512:1024], lo_, ro, start=(hp == 0), stop=(hp == 7),
                       inc=("t" if hp == 7 else None))
                p.wait("vector", "t")
                xs = s_x[:, S * m + t0: S * m + t0 + TT]
                bcol = s_mu[:, 32 + m:33 + m]
                vop(lambda e, xs=xs, bcol=bcol, PP=PP: e.scalar_tensor_tensor(
                    out=s_lntmp, in0=PP[:, 512:1024], scalar=bcol,
                    in1=xs, op0=ALU.add, op1=ALU.add))
                vop(lambda e, xs=xs, PP=PP: e.tensor_add(xs, s_lntmp, PP[:, 0:512]))
                bkp.set(idx)
        barrier()
        free_slots()

    for seq in range(1 if DBG else SEQ_PER_CORE):
        if seq == 0:
            xv = d_x[0].rearrange("(c p) t -> p c t", p=128)
            for c in range(NCH):
                sdma(s_x[:, S * c: S * c + S], xv[:, c, :])
        p.wait("tensor", "s")
        p.wait("vector", "s")
        p.wait("scalar", "s")
        if 0 in SUBLAYERS:
            emit_attn_a()
        if 1 in SUBLAYERS:
            emit_ffn()
        if 2 in SUBLAYERS:
            emit_attn_b(seq)
        if 3 in SUBLAYERS:
            emit_ffn(io_seq=seq)
        barrier()
    p.wait("sync", "s")
    fin = ctrs["s"].n
    p.wait("gpsimd", "s", fin)

    @block.tensor
    def _(e):
        for fn in p.streams["tensor"]:
            fn(e)

    @block.vector
    def _(e):
        for fn in p.streams["vector"]:
            fn(e)

    @block.scalar
    def _(e):
        for fn in p.streams["scalar"]:
            fn(e)

    @block.sync
    def _(e):
        for fn in p.streams["sync"]:
            fn(e)

    @block.gpsimd
    def _(e):
        for fn in p.streams["gpsimd"]:
            fn(e)

    es.close()
    nc.finalize()
    _CACHE["nc"] = nc
    return nc



def _host_inputs(inputs):
    x = np.asarray(inputs["L0_attn_a_hidden_states"], dtype=np.float32)
    xt = np.ascontiguousarray(x.transpose(0, 2, 1))

    def T(a):
        return np.ascontiguousarray(
            np.asarray(a, dtype=np.float32).T).astype(ml_dtypes.bfloat16)

    wq = T(inputs["L0_attn_a_Wq"])
    wk = T(inputs["L0_attn_a_Wk"])
    wv = T(inputs["L0_attn_a_Wv"])
    wo = T(inputs["L0_attn_a_Wo"])
    wr = T(inputs["L0_ffn_W_r"])
    wfk = T(inputs["L0_ffn_W_k"])
    wfv = T(inputs["L0_ffn_W_v"])
    wqkv_raw = T(inputs["L0_attn_b_Wqkv"])
    perm = np.concatenate([np.concatenate([192 * h + off + np.arange(64)
                                           for h in range(16)])
                           for off in (0, 64, 128)])
    wqkv = np.ascontiguousarray(wqkv_raw[:, perm])
    wproj = T(inputs["L0_attn_b_Wproj"])
    wfT = T(inputs["L0_attn_a_Wf"])

    wfbdk = np.zeros((128, 512), dtype=np.float32)
    wfbdk[0:64, 0:256] = wfT
    wfbdk[64:128, 256:512] = wfT
    wfbdq = np.zeros((128, 512), dtype=np.float32)
    for j in range(4):
        wfbdq[0:64, 128 * j:128 * j + 64] = wfT[:, 64 * j:64 * j + 64]
        wfbdq[64:128, 128 * j + 64:128 * j + 128] = wfT[:, 64 * j:64 * j + 64]

    mu = np.zeros((128, 48), dtype=np.float32)
    mur = np.asarray(inputs["L0_ffn_mu_r"], dtype=np.float32).reshape(8, 128).T
    muk = np.asarray(inputs["L0_ffn_mu_k"], dtype=np.float32).reshape(8, 128).T
    bpr = np.asarray(inputs["L0_attn_b_bproj"], dtype=np.float32).reshape(8, 128).T
    mu[:, 0:8] = mur
    mu[:, 8:16] = 1.0 - mur
    mu[:, 16:24] = muk
    mu[:, 24:32] = 1.0 - muk
    mu[:, 32:40] = bpr

    ident = np.eye(128, dtype=ml_dtypes.bfloat16)
    pp, nn = np.meshgrid(np.arange(128), np.arange(128), indexing="ij")
    m01 = ((nn % 8) == (pp % 8)).astype(ml_dtypes.bfloat16)
    ones = np.ones((128, 4), dtype=np.float32)
    sel16 = np.zeros((16, 1024), dtype=np.float32)
    for c in range(8):
        sel16[2 * c, 128 * c:128 * c + 64] = 1.0
        sel16[2 * c + 1, 128 * c + 64:128 * c + 128] = 1.0

    shared = dict(wq=wq, wk=wk, wv=wv, wo=wo, wr=wr, wfk=wfk, wfv=wfv,
                  wqkv=wqkv, wproj=wproj,
                  wfbdk=wfbdk.astype(ml_dtypes.bfloat16),
                  wfbdq=wfbdq.astype(ml_dtypes.bfloat16), mu=mu,
                  ident=ident, m01=m01, onesin=ones, sel16=sel16)
    in_maps = []
    for core in range(NCORES):
        m = dict(shared)
        m["x"] = np.ascontiguousarray(xt[core * SEQ_PER_CORE:(core + 1) * SEQ_PER_CORE])
        in_maps.append(m)
    return in_maps


def kernel(**inputs):
    nc = build_program()
    in_maps = _host_inputs(inputs)
    res = run_bass_kernel_spmd(nc, in_maps, core_ids=list(range(NCORES)))
    outs = [res.results[c]["out"] for c in range(NCORES)]
    full = np.concatenate(outs, axis=0)
    return np.ascontiguousarray(full.transpose(0, 2, 1))


# version 10 (final)
# speedup vs baseline: 1.2494x; 1.0004x over previous
import contextlib
import numpy as np
import ml_dtypes

import concourse.bass as bass
import concourse.mybir as mybir
from concourse.bass_utils import run_bass_kernel_spmd

F32 = mybir.dt.float32
BF16 = mybir.dt.bfloat16
AF = mybir.ActivationFunctionType
ALU = mybir.AluOpType

B, S, D = 32, 2048, 1024
NCORES = 8
SEQ_PER_CORE = B // NCORES
NCH = D // 128
TT = 512
NTILES = S // TT
NSUB = TT // 128
H = 16
XL = 520
EPS = 1e-5

SUBLAYERS = (0, 1, 2, 3)
DBG = False
ATTNB_STAGE = 4


class Ctr:
    __slots__ = ("h", "n")

    def __init__(self, handle):
        self.h = handle
        self.n = 0


class Prog:
    def __init__(self, ctrs):
        self.streams = {e: [] for e in ("tensor", "vector", "scalar", "sync", "gpsimd")}
        self.c = ctrs
        self.last_wait = {}

    def op(self, eng, fn, inc=None, k=1):
        if inc is None:
            self.streams[eng].append(fn)
            return None
        ctr = self.c[inc]
        ctr.n += k
        h = ctr.h
        self.streams[eng].append(lambda e, fn=fn, h=h, k=k: fn(e).then_inc(h, k))
        return ctr.n

    def wait(self, eng, cname, val=None):
        ctr = self.c[cname]
        v = ctr.n if val is None else val
        if v <= 0:
            return
        if eng == cname:
            h = ctr.h
            self.streams[eng].append(lambda e, h=h, v=v: e.wait_ge(h, v))
            return
        key = (eng, cname)
        if self.last_wait.get(key, -1) >= v:
            return
        self.last_wait[key] = v
        h = ctr.h
        self.streams[eng].append(lambda e, h=h, v=v: e.wait_ge(h, v))


_CACHE = {}


def build_program():
    if "nc" in _CACHE:
        return _CACHE["nc"]
    nc = bass.Bass("TRN2", target_bir_lowering=False)
    es = contextlib.ExitStack()
    A = es.enter_context

    d_x = nc.dram_tensor("x", [SEQ_PER_CORE, D, S], F32, kind="ExternalInput").ap()
    d_out = nc.dram_tensor("out", [SEQ_PER_CORE, D, S], F32, kind="ExternalOutput").ap()
    dw = {}
    for nm in ("wq", "wk", "wv", "wo", "wr", "wfk", "wfv", "wproj"):
        dw[nm] = nc.dram_tensor(nm, [D, D], BF16, kind="ExternalInput").ap()
    dw["wqkv"] = nc.dram_tensor("wqkv", [D, 3 * D], BF16, kind="ExternalInput").ap()
    d_wfbdk = nc.dram_tensor("wfbdk", [128, 512], BF16, kind="ExternalInput").ap()
    d_wfbdq = nc.dram_tensor("wfbdq", [128, 512], BF16, kind="ExternalInput").ap()
    d_mu = nc.dram_tensor("mu", [128, 48], F32, kind="ExternalInput").ap()
    d_ident = nc.dram_tensor("ident", [128, 128], BF16, kind="ExternalInput").ap()
    d_m01 = nc.dram_tensor("m01", [128, 128], BF16, kind="ExternalInput").ap()
    d_ones = nc.dram_tensor("onesin", [128, 4], F32, kind="ExternalInput").ap()
    d_sel16 = nc.dram_tensor("sel16", [16, 1024], F32, kind="ExternalInput").ap()

    s_x = A(nc.sbuf_tensor("s_x", [128, NCH * S], F32))
    s_xlnb = A(nc.sbuf_tensor("s_xlnb", [128, NCH * XL], BF16))
    s_ar2 = A(nc.sbuf_tensor("s_ar2", [128, 16384], BF16))
    s_scr = A(nc.sbuf_tensor("s_scr", [128, 16384], BF16))
    s_w = [A(nc.sbuf_tensor(f"s_w{i}", [128, 8192], BF16)) for i in range(3)]
    s_wfbdk = A(nc.sbuf_tensor("s_wfbdk", [128, 512], BF16))
    s_wfbdq = A(nc.sbuf_tensor("s_wfbdq", [128, 512], BF16))
    s_mu = A(nc.sbuf_tensor("s_mu", [128, 48], F32))
    s_ident = A(nc.sbuf_tensor("s_ident", [128, 128], BF16))
    s_m01 = A(nc.sbuf_tensor("s_m01", [128, 128], BF16))
    s_onesb = A(nc.sbuf_tensor("s_onesb", [128, 4], BF16))
    s_onesrow = A(nc.sbuf_tensor("s_onesrow", [1, 128], F32))
    s_sel16 = A(nc.sbuf_tensor("s_sel16", [16, 1024], F32))
    s_xbc = A(nc.sbuf_tensor("s_xbc", [128, 2048], BF16))
    xbcF = s_xbc[:, :].bitcast(F32)
    s_lntmp = xbcF[:, 0:512]
    s_rowA = xbcF[0:1, 0:512]
    s_rowC = xbcF[0:1, 512:1024]
    s_rn16 = s_scr[0:16, 14848:15872].bitcast(F32)
    s_rn8 = A(nc.sbuf_tensor("s_rn8", [128, 8], F32))
    s_cst = A(nc.sbuf_tensor("s_cst", [128, 4], F32))
    s_expb = A(nc.sbuf_tensor("s_expb", [128, 1024], BF16))
    s_v8 = A(nc.sbuf_tensor("s_v8", [128, 8 * 65], BF16))
    s_outx = A(nc.sbuf_tensor("s_outx", [128, 8 * 64], BF16))
    s_low = A(nc.sbuf_tensor("s_low", [128, 4096], BF16))

    PA = A(nc.psum_tensor("p_a", [128, 512], F32))
    PB = A(nc.psum_tensor("p_b", [128, 512], F32))
    KVA = A(nc.psum_tensor("p_kva", [128, 1024], F32))
    KVB = A(nc.psum_tensor("p_kvb", [128, 1024], F32))
    KVC = A(nc.psum_tensor("p_kvc", [128, 1024], F32))

    ctrs = {n: Ctr(A(nc.semaphore("c_" + n))) for n in ("t", "v", "a", "s", "g")}
    block = A(nc.Block())
    p = Prog(ctrs)

    def kv_slice(sl):
        t = KVA if sl < 16 else KVB
        o = 64 * (sl % 16)
        return t[:, o:o + 64]

    oKT, oKF, oVT = 0, 4096, 8192
    oQF, oO, oBD, oNT = 0, 8192, 10240, 14336
    oXR, oXK, oKA = 0, 4096, 8192
    oQ, oK, oV = 0, 4096, 8192

    def gdma(dst, src):
        p.op("gpsimd", lambda e, d=dst, s_=src: e.dma_start(out=d, in_=s_), inc="g", k=16)

    def sdma(dst, src):
        p.op("sync", lambda e, d=dst, s_=src: e.dma_start(out=d, in_=s_), inc="s", k=16)

    def mm(out, lhsT, rhs, start, stop, inc=None, tile_position=None):
        p.op("tensor",
             lambda e, o=out, l=lhsT, r=rhs, st=start, sp=stop, tp=tile_position:
             e.matmul(o, l, r, start=st, stop=sp, tile_position=tp,
                      skip_group_check=True),
             inc=inc)

    def ttrans(out, in_, ident, inc=None, tile_position=None):
        p.op("tensor",
             lambda e, o=out, i=in_, d=ident, tp=tile_position:
             e.transpose(o, i, d, tile_position=tp),
             inc=inc)

    def vop(fn):
        p.op("vector", fn, inc="v")

    def aop(fn):
        p.op("scalar", fn, inc="a")

    def tmark():
        mm(PA[0:1, 0:1], s_onesb[0:1, 0:1], s_onesb[0:1, 0:1], True, True, inc="t")
        return ctrs["t"].n

    def barrier():
        tn = tmark()
        vn, an = ctrs["v"].n, ctrs["a"].n
        p.wait("tensor", "v", vn)
        p.wait("tensor", "a", an)
        p.wait("vector", "t", tn)
        p.wait("vector", "a", an)
        p.wait("scalar", "t", tn)
        p.wait("scalar", "v", vn)

    gdma(s_mu[:], d_mu[:])
    gdma(s_ident[:], d_ident[:])
    gdma(s_m01[:], d_m01[:])
    gdma(s_onesb[:], d_ones[:])
    gdma(s_sel16[:], d_sel16[:])
    gdma(s_wfbdk[:], d_wfbdk[:])
    gdma(s_wfbdq[:], d_wfbdq[:])
    g_const = ctrs["g"].n
    p.wait("vector", "g", g_const)
    vop(lambda e: e.memset(s_onesrow[:], 1.0))
    vop(lambda e: e.memset(s_cst[:, 0:1], -10.0))
    vop(lambda e: e.memset(s_cst[:, 1:2], EPS))
    vop(lambda e: e.memset(
        s_v8.rearrange("p (g c) -> p g c", g=8)[:, :, 64:65], 1.0))
    ONESB = s_onesb[:, 0:1]

    slot_loaded = [0, 0, 0]
    slot_freed = [0, 0, 0]
    slot_rr = [0]

    def load_weight(dram_ap):
        slot = slot_rr[0]
        slot_rr[0] = (slot + 1) % 3
        p.wait("gpsimd", "t", slot_freed[slot])
        src = dram_ap.rearrange("(c p) f -> p c f", p=128)
        dst = s_w[slot].rearrange("p (c f) -> p c f", c=8)
        gdma(dst, src)
        slot_loaded[slot] = ctrs["g"].n
        return slot

    def free_slot(slot):
        slot_freed[slot] = ctrs["t"].n

    def wtile(slot, c, m):
        return s_w[slot][:, 1024 * c + 128 * m: 1024 * c + 128 * m + 128]

    def emit_ln(tile):
        t0 = tile * TT
        marks = {}
        for c in range(NCH):
            pp = 1024 * (c % 2)
            xb = s_xbc[:, pp: pp + 512]
            xq = s_xbc[:, pp + 512: pp + 1024]
            if c >= 2:
                p.wait("vector", "t", marks[c - 2])
            xs = s_x[:, S * c + t0: S * c + t0 + TT]
            vop(lambda e, xb=xb, xs=xs: e.tensor_copy(out=xb, in_=xs))
            vop(lambda e, xq=xq, xb=xb: e.tensor_mul(xq, xb, xb))
            p.wait("tensor", "v")
            mm(PA[0:1, :], ONESB, xb, start=(c == 0), stop=(c == NCH - 1))
            mm(PB[0:1, :], ONESB, xq, start=(c == 0), stop=(c == NCH - 1), inc="t")
            marks[c] = ctrs["t"].n
        p.wait("vector", "t")
        rA, rC = s_rowA, s_rowC
        vop(lambda e: e.tensor_scalar_mul(rA, PA[0:1, :], 1.0 / D))
        vop(lambda e: e.tensor_mul(rC, rA, rA))
        vop(lambda e: e.scalar_tensor_tensor(out=PA[0:1, :], in0=PB[0:1, :],
                                             scalar=1.0 / D, in1=rC,
                                             op0=ALU.mult, op1=ALU.subtract))
        p.wait("scalar", "v")
        aop(lambda e: e.activation(out=rC, in_=PA[0:1, :], func=AF.Ln,
                                   bias=s_cst[0:1, 1:2]))
        aop(lambda e: e.activation(out=rC, in_=rC, func=AF.Exp, scale=-0.5))
        p.wait("vector", "a")
        vop(lambda e: e.tensor_mul(rA, rA, rC))
        p.wait("tensor", "v")
        p.wait("tensor", "a")
        mm(PA[:], s_onesrow[:], rC, start=True, stop=True)
        mm(PB[:], s_onesrow[:], rA, start=True, stop=True, inc="t")
        p.wait("vector", "t")
        for c in range(NCH):
            xs = s_x[:, S * c + t0: S * c + t0 + TT]
            dst = s_xlnb[:, XL * c + 1: XL * c + 1 + TT]
            vop(lambda e, xs=xs: e.tensor_mul(s_lntmp, xs, PA[:]))
            vop(lambda e, dst=dst: e.tensor_sub(dst, s_lntmp, PB[:]))

    def xlnb_c(c, lo, n):
        return s_xlnb[:, XL * c + 1 + lo: XL * c + 1 + lo + n]

    class BankMarks:
        def __init__(self, n=2):
            self.m = [None] * n

        def wait(self, bank):
            mk = self.m[bank]
            if mk is None:
                p.wait("tensor", "v")
                p.wait("tensor", "a")
            else:
                p.wait("tensor", "v", mk[0])
                p.wait("tensor", "a", mk[1])

        def set(self, bank):
            self.m[bank] = (ctrs["v"].n, ctrs["a"].n)

    def emit_proj(slot, rhs_fn, evac, w=512):
        bk = BankMarks()
        for m in range(NCH):
            ps = (PA if m % 2 == 0 else PB)[:, 0:w]
            bk.wait(m % 2)
            for c in range(NCH):
                mm(ps, wtile(slot, c, m), rhs_fn(c), start=(c == 0), stop=(c == NCH - 1),
                   inc=("t" if c == NCH - 1 else None))
            evac(m, ps)
            bk.set(m % 2)

    def elu_evac(ps, dst, w=512):
        p.wait("vector", "t")
        t1 = s_xbc[:, 0:w]
        t3 = s_xbc[:, 1024:1024 + w]
        vop(lambda e, t1=t1, ps=ps: e.tensor_scalar_min(t1, ps, 0.0))
        p.wait("scalar", "t")
        p.wait("scalar", "v")
        aop(lambda e, t1=t1: e.activation(out=t1, in_=t1, func=AF.Exp))
        aop(lambda e, t3=t3, ps=ps: e.activation(out=t3, in_=ps, func=AF.Relu))
        p.wait("vector", "a")
        vop(lambda e, dst=dst, t1=t1, t3=t3: e.tensor_add(dst, t1, t3))

    def emit_attn_a():
        QS = s_ar2
        sq = load_weight(dw["wq"])
        sk = load_weight(dw["wk"])
        sv = load_weight(dw["wv"])
        so = [None]
        for tile in range(NTILES):
            emit_ln(tile)
            p.wait("tensor", "g", slot_loaded[sq])

            def ev_q(m, ps, tile=tile):
                p.wait("vector", "t")
                dst = QS[:, 2048 * m + TT * tile: 2048 * m + TT * tile + TT]
                vop(lambda e, dst=dst, ps=ps: e.tensor_copy(out=dst, in_=ps))
            emit_proj(sq, lambda c: xlnb_c(c, 0, TT), ev_q)
            if tile == NTILES - 1:
                free_slot(sq)
                so[0] = load_weight(dw["wo"])

            p.wait("tensor", "g", slot_loaded[sk])

            def ev_k(m, ps):
                p.wait("vector", "t")
                dst = s_scr[:, oKT + 512 * m: oKT + 512 * m + TT]
                vop(lambda e, dst=dst, ps=ps: e.tensor_copy(out=dst, in_=ps))
            emit_proj(sk, lambda c: xlnb_c(c, 0, TT), ev_k)
            if tile == NTILES - 1:
                free_slot(sk)
            p.wait("tensor", "g", slot_loaded[sv])

            for sub in range(NSUB):
                lo = 128 * sub
                p.wait("tensor", "v")
                for c in range(NCH):
                    lhsT = xlnb_c(c, lo, 128)
                    mm(PA[:], lhsT, s_w[sv][:, 1024 * c: 1024 * c + 512],
                       start=(c == 0), stop=(c == NCH - 1))
                    mm(PB[:], lhsT, s_w[sv][:, 1024 * c + 512: 1024 * c + 1024],
                       start=(c == 0), stop=(c == NCH - 1),
                       inc=("t" if c == NCH - 1 else None))
                p.wait("vector", "t")
                for half, ps in ((0, PA[:]), (1, PB[:])):
                    dst = s_scr[:, oVT + 512 * half: oVT + 512 * half + 512]
                    vop(lambda e, dst=dst, ps=ps: e.tensor_copy(out=dst, in_=ps))
                bkk = BankMarks()
                for c in range(NCH):
                    ps = (PA if c % 2 == 0 else PB)[:]
                    bkk.wait(c % 2)
                    mm(ps, s_scr[:, oKT + 512 * c + lo: oKT + 512 * c + lo + 128],
                       s_wfbdk[:], start=True, stop=True, inc="t")
                    elu_evac(ps, s_scr[:, oKF + 512 * c: oKF + 512 * c + 512])
                    bkk.set(c % 2)
                p.wait("tensor", "v")
                first = (tile == 0 and sub == 0)
                last = (tile == NTILES - 1 and sub == NSUB - 1)
                for h in range(H):
                    for ph in range(2):
                        sl = 2 * h + ph
                        o = oKF + 512 * (h // 2) + 256 * (h % 2) + 128 * ph
                        mm(kv_slice(sl), s_scr[:, o:o + 128],
                           s_scr[:, oVT + 64 * h: oVT + 64 * h + 64],
                           start=first, stop=last)
                        mm(KVC[:, 2 * sl:2 * sl + 1], s_scr[:, o:o + 128],
                           ONESB, start=first, stop=last,
                           inc=("t" if sl == 31 else None))
            barrier()
            if tile == NTILES - 1:
                free_slot(sv)

        p.wait("vector", "t")
        for c in range(NCH):
            for j in range(4):
                idx = 4 * c + j
                bd = s_scr[:, oBD + 128 * idx: oBD + 128 * idx + 128]
                ntl = s_scr[:, oNT + 16 * idx: oNT + 16 * idx + 16]
                vop(lambda e, bd=bd: e.memset(bd, 0.0))
                vop(lambda e, ntl=ntl: e.memset(ntl, 0.0))
                for par in range(2):
                    h = 2 * c + par
                    sl = 2 * h + (j // 2)
                    src = kv_slice(sl)[64 * (j % 2):64 * (j % 2) + 64, :]
                    ks = KVC[64 * (j % 2):64 * (j % 2) + 64, 2 * sl:2 * sl + 1]
                    vop(lambda e, bd=bd, src=src, par=par:
                        e.tensor_copy(out=bd[64 * par:64 * par + 64,
                                             64 * par:64 * par + 64],
                                      in_=src))
                    vop(lambda e, ntl=ntl, ks=ks, par=par, h=h:
                        e.tensor_copy(out=ntl[64 * par:64 * par + 64, h:h + 1],
                                      in_=ks))
        barrier()

        swo = so[0]
        HT = 256
        for ht in range(S // HT):
            tb = HT * ht
            bkq = BankMarks()
            for c in range(NCH):
                for j in range(4):
                    idx = 4 * c + j
                    ps = (PA if idx % 2 == 0 else PB)[:, 0:HT]
                    bkq.wait(idx % 2)
                    mm(ps, s_wfbdq[:, 128 * j:128 * j + 128],
                       QS[:, 2048 * c + tb: 2048 * c + tb + HT],
                       start=True, stop=True, inc="t")
                    elu_evac(ps, s_scr[:, oQF + 256 * idx: oQF + 256 * idx + HT], w=HT)
                    bkq.set(idx % 2)
            p.wait("tensor", "v")
            for idx in range(32):
                mm(KVA[0:16, 0:HT], s_scr[:, oNT + 16 * idx: oNT + 16 * idx + 16],
                   s_scr[:, oQF + 256 * idx: oQF + 256 * idx + HT],
                   start=(idx == 0), stop=(idx == 31),
                   inc=("t" if idx == 31 else None))
            p.wait("vector", "t")
            rn = s_rn16[:, 0:HT]
            vop(lambda e, rn=rn: e.tensor_scalar_add(rn, KVA[0:16, 0:HT], 1e-6))
            vop(lambda e, rn=rn: e.reciprocal(out=rn, in_=rn))
            for c in range(NCH):
                p.wait("tensor", "v")
                mm(KVC[:, 0:HT], s_sel16[:, 128 * c:128 * c + 128], s_rn16[:, 0:HT],
                   start=True, stop=True)
                for j in range(4):
                    idx = 4 * c + j
                    mm(KVB[:, 0:HT], s_scr[:, oBD + 128 * idx: oBD + 128 * idx + 128],
                       s_scr[:, oQF + 256 * idx: oQF + 256 * idx + HT],
                       start=(j == 0), stop=(j == 3), inc=("t" if j == 3 else None))
                p.wait("vector", "t")
                dst = s_scr[:, oO + 256 * c: oO + 256 * c + HT]
                vop(lambda e: e.tensor_copy(out=s_lntmp[:, 0:HT], in_=KVC[:, 0:HT]))
                vop(lambda e, dst=dst: e.tensor_mul(dst, KVB[:, 0:HT],
                                                    s_lntmp[:, 0:HT]))
            p.wait("tensor", "g", slot_loaded[swo])

            def ev_o(m, ps, tb=tb):
                p.wait("vector", "t")
                xs = s_x[:, S * m + tb: S * m + tb + HT]
                vop(lambda e, xs=xs, ps=ps: e.tensor_add(xs, xs, ps))
            emit_proj(swo, lambda c: s_scr[:, oO + 256 * c: oO + 256 * c + HT],
                      ev_o, w=HT)
            barrier()
        free_slot(swo)

    def emit_ffn(io_seq=None):
        fr = load_weight(dw["wr"])
        fk = load_weight(dw["wfk"])
        fv = load_weight(dw["wfv"])
        for tile in range(NTILES):
            emit_ln(tile)
            if tile == 0:
                for c in range(NCH):
                    vop(lambda e, c=c: e.memset(s_xlnb[:, XL * c:XL * c + 1], 0.0))
            for c in range(NCH):
                cur = s_xlnb[:, XL * c + 1: XL * c + 1 + TT]
                prev = s_xlnb[:, XL * c: XL * c + TT]
                for (oDST, mo, omo) in ((oXR, 0, 8), (oXK, 16, 24)):
                    t1 = s_xbc[:, 1024 * (c % 2): 1024 * (c % 2) + 512]
                    dst = s_scr[:, oDST + 512 * c: oDST + 512 * c + 512]
                    vop(lambda e, t1=t1, prev=prev, c=c, o=omo:
                        e.tensor_scalar_mul(t1, prev, s_mu[:, o + c:o + c + 1]))
                    vop(lambda e, dst=dst, cur=cur, t1=t1, c=c, o=mo:
                        e.scalar_tensor_tensor(out=dst, in0=cur,
                                               scalar=s_mu[:, o + c:o + c + 1],
                                               in1=t1, op0=ALU.mult, op1=ALU.add))
            for c in range(NCH):
                vop(lambda e, c=c: e.tensor_copy(
                    out=s_xlnb[:, XL * c:XL * c + 1],
                    in_=s_xlnb[:, XL * c + TT:XL * c + TT + 1]))
            p.wait("tensor", "g", slot_loaded[fr])

            def ev_r(m, ps):
                p.wait("scalar", "t")
                dst = s_ar2[:, 1024 * m: 1024 * m + 1024].bitcast(F32)
                aop(lambda e, dst=dst, ps=ps: e.activation(out=dst, in_=ps,
                                                           func=AF.Sigmoid))
            emit_proj(fr, lambda c: s_scr[:, oXR + 512 * c: oXR + 512 * c + 512], ev_r)
            if tile == NTILES - 1:
                free_slot(fr)
            p.wait("tensor", "g", slot_loaded[fk])

            def ev_k(m, ps):
                p.wait("scalar", "t")
                p.wait("scalar", "v")
                t1 = s_xbc[:, 1024 * (m % 2): 1024 * (m % 2) + 512]
                dst = s_scr[:, oKA + 512 * m: oKA + 512 * m + 512]
                aop(lambda e, t1=t1, ps=ps: e.activation(out=t1, in_=ps, func=AF.Relu))
                p.wait("vector", "a")
                vop(lambda e, dst=dst, t1=t1: e.tensor_mul(dst, t1, t1))
            emit_proj(fk, lambda c: s_scr[:, oXK + 512 * c: oXK + 512 * c + 512], ev_k)
            if tile == NTILES - 1:
                free_slot(fk)
            p.wait("tensor", "g", slot_loaded[fv])
            t0 = tile * TT

            def ev_v(m, ps, t0=t0, tile=tile):
                p.wait("vector", "t")
                p.wait("vector", "a")
                r = s_ar2[:, 1024 * m: 1024 * m + 1024].bitcast(F32)
                xs = s_x[:, S * m + t0: S * m + t0 + TT]
                vop(lambda e, r=r, ps=ps: e.tensor_mul(s_lntmp, r, ps))
                vop(lambda e, xs=xs: e.tensor_add(xs, xs, s_lntmp))
                if io_seq is not None and tile == NTILES - 1:
                    p.wait("sync", "v")
                    ov = d_out[io_seq].rearrange("(c p) t -> p c t", p=128)
                    sdma(ov[:, m, :], s_x[:, S * m: S * m + S])
                    if io_seq + 1 < SEQ_PER_CORE:
                        p.wait("sync", "s")
                        xv = d_x[io_seq + 1].rearrange("(c p) t -> p c t", p=128)
                        sdma(s_x[:, S * m: S * m + S], xv[:, m, :])
            emit_proj(fv, lambda c: s_scr[:, oKA + 512 * c: oKA + 512 * c + 512], ev_v)
            barrier()
            if tile == NTILES - 1:
                free_slot(fv)

    def emit_attn_b(seqidx=0):
        OD = s_ar2
        bq = load_weight(dw["wqkv"][:, 0:D])
        bk = load_weight(dw["wqkv"][:, D:2 * D])
        bv = load_weight(dw["wqkv"][:, 2 * D:3 * D])
        bslots = (bq, bk, bv)
        wp = [None]
        qlow = s_low[0:64, 0:2048]
        klow = s_low[0:64, 2048:4096]
        vlow = s_low[64:128, 0:2048]
        for tile in range(NTILES):
            emit_ln(tile)
            p.wait("vector", "s")
            for ki, oDst in ((0, oQ), (1, oK), (2, oV)):
                kslot = bslots[ki]
                p.wait("tensor", "g", slot_loaded[kslot])
                reg = s_scr[:, oDst:oDst + 4096].rearrange("p (g w) -> p g w", g=64)

                def ev(m, ps, reg=reg):
                    p.wait("vector", "t")
                    dst = reg[:, :, 8 * m:8 * m + 8]
                    psv = ps.rearrange("p (g t) -> p g t", g=64)
                    vop(lambda e, dst=dst, psv=psv: e.tensor_copy(out=dst, in_=psv))
                emit_proj(kslot, lambda c: xlnb_c(c, 0, TT), ev)
                if tile == NTILES - 1:
                    free_slot(kslot)
                    if ki == 0:
                        wp[0] = load_weight(dw["wproj"])
            for sub in range(NSUB):
                lo = 128 * sub
                p.wait("sync", "v")
                p.wait("sync", "t")
                for oSrc, lowt in ((oQ, qlow), (oK, klow), (oV, vlow)):
                    reg = s_scr[:, oSrc:oSrc + 4096].rearrange("p (g w) -> p g w", g=64)
                    se = reg[0:64, 16 * sub:16 * sub + 16, :]
                    so = reg[64:128, 16 * sub:16 * sub + 16, :]
                    lv = lowt.rearrange("p (g par w) -> p g par w", g=16, par=2)
                    sdma(lv[:, :, 0, :], se)
                    sdma(lv[:, :, 1, :], so)
                p.wait("tensor", "s")
                for gb in range(2):
                    p.wait("tensor", "a")
                    for g8 in range(8):
                        grp = 8 * gb + g8
                        mm(KVC[:, 128 * g8:128 * g8 + 128],
                           klow[:, 128 * grp:128 * grp + 128],
                           qlow[:, 128 * grp:128 * grp + 128],
                           start=True, stop=True, inc=("t" if g8 == 7 else None))
                    p.wait("scalar", "t")
                    p.wait("scalar", "v")
                    aop(lambda e: e.activation(out=s_expb[:], in_=KVC[:, 0:1024],
                                               func=AF.Exp, bias=s_cst[:, 0:1], scale=0.125))
                    p.wait("vector", "a")
                    for g8 in range(8):
                        ev_ = s_expb[:, 128 * g8:128 * g8 + 128]
                        vop(lambda e, ev_=ev_: e.tensor_tensor(
                            out=ev_, in0=ev_, in1=s_m01[:], op=ALU.mult))
                    if ATTNB_STAGE < 2:
                        continue
                    p.wait("tensor", "v")
                    for g8 in range(8):
                        grp = 8 * gb + g8
                        pslot = PA[:, 32 * g8:32 * g8 + 32].bitcast(BF16)
                        ttrans(pslot, vlow[:, 128 * grp:128 * grp + 128],
                               s_ident[64:128, 64:128],
                               tile_position=(64, 0),
                               inc=("t" if g8 == 7 else None))
                    p.wait("scalar", "t")
                    for g8 in range(8):
                        pslot = PA[:, 32 * g8:32 * g8 + 32].bitcast(BF16)
                        dst = s_v8[:, 65 * g8:65 * g8 + 64]
                        aop(lambda e, dst=dst, pslot=pslot: e.activation(
                            out=dst, in_=pslot, func=AF.Copy))
                    if ATTNB_STAGE < 3:
                        continue
                    p.wait("tensor", "a")
                    p.wait("tensor", "v")
                    for g8 in range(8):
                        mm(KVA[:, 128 * g8:128 * g8 + 65],
                           s_expb[:, 128 * g8:128 * g8 + 128],
                           s_v8[:, 65 * g8:65 * g8 + 65],
                           start=True, stop=True, inc=("t" if g8 == 7 else None))
                    p.wait("vector", "t")
                    kvv = KVA[:, 0:1024].rearrange("p (g c) -> p g c", g=8)
                    vop(lambda e, kvv=kvv: e.reciprocal(out=s_rn8[:],
                                                        in_=kvv[:, :, 64:65]))
                    p.wait("vector", "v")
                    for g8 in range(8):
                        ov = s_outx[:, 64 * g8:64 * g8 + 64]
                        kslc = KVA[:, 128 * g8:128 * g8 + 64]
                        vop(lambda e, ov=ov, kslc=kslc, g8=g8: e.tensor_scalar_mul(
                            ov, kslc, s_rn8[:, g8:g8 + 1]))
                    if ATTNB_STAGE < 4:
                        continue
                    p.wait("tensor", "v")
                    for g8 in range(8):
                        pslot = PA[:, 256 + 32 * g8:256 + 32 * g8 + 32].bitcast(BF16)
                        ine = s_outx[0:64, 64 * g8:64 * g8 + 64]
                        ino = s_outx[64:128, 64 * g8:64 * g8 + 64]
                        ttrans(pslot[0:64, :], ine, s_ident[0:64, 0:64])
                        ttrans(pslot[64:128, :], ino, s_ident[64:128, 64:128],
                               tile_position=(64, 64),
                               inc=("t" if g8 == 7 else None))
                    p.wait("scalar", "t")
                    gbase = (tile * TT + lo) // 8 + 8 * gb
                    dst = OD[:, 64 * gbase: 64 * gbase + 512]
                    psrc = PA[:, 256:512].bitcast(BF16)
                    aop(lambda e, dst=dst, psrc=psrc: e.activation(
                        out=dst, in_=psrc, func=AF.Copy))
                    if DBG and seqidx == 0 and tile == 0 and sub == 0 and gb == 0:
                        dbf = s_scr[:, 12288:14336]
                        dbF = dbf.bitcast(F32)
                        p.wait("vector", "t")
                        p.wait("vector", "a")
                        vop(lambda e: e.tensor_copy(out=dbF[:, 0:512],
                                                    in_=KVC[:, 0:512]))
                        vop(lambda e: e.tensor_copy(out=dbF[:, 512:640],
                                                    in_=s_expb[:, 0:128]))
                        vop(lambda e: e.tensor_copy(out=dbF[:, 640:705],
                                                    in_=s_v8[:, 0:65]))
                        vop(lambda e: e.tensor_copy(out=dbF[:, 705:769],
                                                    in_=s_outx[:, 0:64]))
                        vop(lambda e: e.tensor_copy(out=dbF[:, 769:834],
                                                    in_=KVA[:, 0:65]))
                        vop(lambda e: e.tensor_copy(out=dbF[:, 834:898],
                                                    in_=OD[:, 0:64]))
                        p.wait("sync", "v")
                        dv = d_out[0].rearrange("(c p) t -> p c t", p=128)
                        sdma(dv[:, 0, 0:1024], dbF[:, 0:1024])
            barrier()
        if ATTNB_STAGE < 4:
            return
        swp = wp[0]
        p.wait("tensor", "g", slot_loaded[swp])
        odv = OD.rearrange("p (G c) -> p G c", G=256)
        bkp = BankMarks()
        for tile in range(NTILES):
            t0 = tile * TT
            for m in range(NCH):
                idx = (tile * NCH + m) % 2
                PP = KVB if idx == 0 else KVA
                bkp.wait(idx)
                for hp in range(8):
                    le = s_w[swp][0:64, 1024 * hp + 128 * m: 1024 * hp + 128 * m + 128]
                    lo_ = s_w[swp][64:128, 1024 * hp + 128 * m: 1024 * hp + 128 * m + 128]
                    re = odv[0:64, 64 * tile:64 * tile + 64, 8 * hp:8 * hp + 8]
                    ro = odv[64:128, 64 * tile:64 * tile + 64, 8 * hp:8 * hp + 8]
                    mm(PP[:, 0:512], le, re, start=(hp == 0), stop=(hp == 7))
                    mm(PP[:, 512:1024], lo_, ro, start=(hp == 0), stop=(hp == 7),
                       inc=("t" if hp == 7 else None))
                p.wait("vector", "t")
                xs = s_x[:, S * m + t0: S * m + t0 + TT]
                bcol = s_mu[:, 32 + m:33 + m]
                vop(lambda e, xs=xs, bcol=bcol, PP=PP: e.scalar_tensor_tensor(
                    out=s_lntmp, in0=PP[:, 512:1024], scalar=bcol,
                    in1=xs, op0=ALU.add, op1=ALU.add))
                vop(lambda e, xs=xs, PP=PP: e.tensor_add(xs, s_lntmp, PP[:, 0:512]))
                bkp.set(idx)
        barrier()
        free_slot(swp)

    for seq in range(1 if DBG else SEQ_PER_CORE):
        if seq == 0:
            xv = d_x[0].rearrange("(c p) t -> p c t", p=128)
            for c in range(NCH):
                sdma(s_x[:, S * c: S * c + S], xv[:, c, :])
        p.wait("tensor", "s")
        p.wait("vector", "s")
        p.wait("scalar", "s")
        if 0 in SUBLAYERS:
            emit_attn_a()
        if 1 in SUBLAYERS:
            emit_ffn()
        if 2 in SUBLAYERS:
            emit_attn_b(seq)
        if 3 in SUBLAYERS:
            emit_ffn(io_seq=seq)
        barrier()
    p.wait("sync", "s")
    fin = ctrs["s"].n
    p.wait("gpsimd", "s", fin)

    @block.tensor
    def _(e):
        for fn in p.streams["tensor"]:
            fn(e)

    @block.vector
    def _(e):
        for fn in p.streams["vector"]:
            fn(e)

    @block.scalar
    def _(e):
        for fn in p.streams["scalar"]:
            fn(e)

    @block.sync
    def _(e):
        for fn in p.streams["sync"]:
            fn(e)

    @block.gpsimd
    def _(e):
        for fn in p.streams["gpsimd"]:
            fn(e)

    es.close()
    nc.finalize()
    _CACHE["nc"] = nc
    return nc



def _host_inputs(inputs):
    x = np.asarray(inputs["L0_attn_a_hidden_states"], dtype=np.float32)
    xt = np.ascontiguousarray(x.transpose(0, 2, 1))

    def T(a):
        return np.ascontiguousarray(
            np.asarray(a, dtype=np.float32).T).astype(ml_dtypes.bfloat16)

    wq = T(inputs["L0_attn_a_Wq"])
    wk = T(inputs["L0_attn_a_Wk"])
    wv = T(inputs["L0_attn_a_Wv"])
    wo = T(inputs["L0_attn_a_Wo"])
    wr = T(inputs["L0_ffn_W_r"])
    wfk = T(inputs["L0_ffn_W_k"])
    wfv = T(inputs["L0_ffn_W_v"])
    wqkv_raw = T(inputs["L0_attn_b_Wqkv"])
    perm = np.concatenate([np.concatenate([192 * h + off + np.arange(64)
                                           for h in range(16)])
                           for off in (0, 64, 128)])
    wqkv = np.ascontiguousarray(wqkv_raw[:, perm])
    wproj = T(inputs["L0_attn_b_Wproj"])
    wfT = T(inputs["L0_attn_a_Wf"])

    wfbdk = np.zeros((128, 512), dtype=np.float32)
    wfbdk[0:64, 0:256] = wfT
    wfbdk[64:128, 256:512] = wfT
    wfbdq = np.zeros((128, 512), dtype=np.float32)
    for j in range(4):
        wfbdq[0:64, 128 * j:128 * j + 64] = wfT[:, 64 * j:64 * j + 64]
        wfbdq[64:128, 128 * j + 64:128 * j + 128] = wfT[:, 64 * j:64 * j + 64]

    mu = np.zeros((128, 48), dtype=np.float32)
    mur = np.asarray(inputs["L0_ffn_mu_r"], dtype=np.float32).reshape(8, 128).T
    muk = np.asarray(inputs["L0_ffn_mu_k"], dtype=np.float32).reshape(8, 128).T
    bpr = np.asarray(inputs["L0_attn_b_bproj"], dtype=np.float32).reshape(8, 128).T
    mu[:, 0:8] = mur
    mu[:, 8:16] = 1.0 - mur
    mu[:, 16:24] = muk
    mu[:, 24:32] = 1.0 - muk
    mu[:, 32:40] = bpr

    ident = np.eye(128, dtype=ml_dtypes.bfloat16)
    pp, nn = np.meshgrid(np.arange(128), np.arange(128), indexing="ij")
    m01 = ((nn % 8) == (pp % 8)).astype(ml_dtypes.bfloat16)
    ones = np.ones((128, 4), dtype=np.float32)
    sel16 = np.zeros((16, 1024), dtype=np.float32)
    for c in range(8):
        sel16[2 * c, 128 * c:128 * c + 64] = 1.0
        sel16[2 * c + 1, 128 * c + 64:128 * c + 128] = 1.0

    shared = dict(wq=wq, wk=wk, wv=wv, wo=wo, wr=wr, wfk=wfk, wfv=wfv,
                  wqkv=wqkv, wproj=wproj,
                  wfbdk=wfbdk.astype(ml_dtypes.bfloat16),
                  wfbdq=wfbdq.astype(ml_dtypes.bfloat16), mu=mu,
                  ident=ident, m01=m01, onesin=ones, sel16=sel16)
    in_maps = []
    for core in range(NCORES):
        m = dict(shared)
        m["x"] = np.ascontiguousarray(xt[core * SEQ_PER_CORE:(core + 1) * SEQ_PER_CORE])
        in_maps.append(m)
    return in_maps


def kernel(**inputs):
    nc = build_program()
    in_maps = _host_inputs(inputs)
    res = run_bass_kernel_spmd(nc, in_maps, core_ids=list(range(NCORES)))
    outs = [res.results[c]["out"] for c in range(NCORES)]
    full = np.concatenate(outs, axis=0)
    return np.ascontiguousarray(full.transpose(0, 2, 1))


# version 12
# speedup vs baseline: 1.2508x; 1.0012x over previous
import contextlib
import numpy as np
import ml_dtypes

import concourse.bass as bass
import concourse.mybir as mybir
from concourse.bass_utils import run_bass_kernel_spmd

F32 = mybir.dt.float32
BF16 = mybir.dt.bfloat16
AF = mybir.ActivationFunctionType
ALU = mybir.AluOpType

B, S, D = 32, 2048, 1024
NCORES = 8
SEQ_PER_CORE = B // NCORES
NCH = D // 128
TT = 512
NTILES = S // TT
NSUB = TT // 128
H = 16
XL = 520
EPS = 1e-5

SUBLAYERS = (0, 1, 2, 3)
DBG = False
ATTNB_STAGE = 4


class Ctr:
    __slots__ = ("h", "n")

    def __init__(self, handle):
        self.h = handle
        self.n = 0


class Prog:
    def __init__(self, ctrs):
        self.streams = {e: [] for e in ("tensor", "vector", "scalar", "sync", "gpsimd")}
        self.c = ctrs
        self.last_wait = {}

    def op(self, eng, fn, inc=None, k=1):
        if inc is None:
            self.streams[eng].append(fn)
            return None
        ctr = self.c[inc]
        ctr.n += k
        h = ctr.h
        self.streams[eng].append(lambda e, fn=fn, h=h, k=k: fn(e).then_inc(h, k))
        return ctr.n

    def wait(self, eng, cname, val=None):
        ctr = self.c[cname]
        v = ctr.n if val is None else val
        if v <= 0:
            return
        if eng == cname:
            h = ctr.h
            self.streams[eng].append(lambda e, h=h, v=v: e.wait_ge(h, v))
            return
        key = (eng, cname)
        if self.last_wait.get(key, -1) >= v:
            return
        self.last_wait[key] = v
        h = ctr.h
        self.streams[eng].append(lambda e, h=h, v=v: e.wait_ge(h, v))


_CACHE = {}


def build_program():
    if "nc" in _CACHE:
        return _CACHE["nc"]
    nc = bass.Bass("TRN2", target_bir_lowering=False)
    es = contextlib.ExitStack()
    A = es.enter_context

    d_x = nc.dram_tensor("x", [SEQ_PER_CORE, D, S], F32, kind="ExternalInput").ap()
    d_out = nc.dram_tensor("out", [SEQ_PER_CORE, D, S], F32, kind="ExternalOutput").ap()
    dw = {}
    for nm in ("wq", "wk", "wv", "wo", "wr", "wfk", "wfv", "wproj"):
        dw[nm] = nc.dram_tensor(nm, [D, D], BF16, kind="ExternalInput").ap()
    dw["wqkv"] = nc.dram_tensor("wqkv", [D, 3 * D], BF16, kind="ExternalInput").ap()
    d_wfbdk = nc.dram_tensor("wfbdk", [128, 512], BF16, kind="ExternalInput").ap()
    d_wfbdq = nc.dram_tensor("wfbdq", [128, 512], BF16, kind="ExternalInput").ap()
    d_mu = nc.dram_tensor("mu", [128, 48], F32, kind="ExternalInput").ap()
    d_ident = nc.dram_tensor("ident", [128, 128], BF16, kind="ExternalInput").ap()
    d_m01 = nc.dram_tensor("m01", [128, 128], BF16, kind="ExternalInput").ap()
    d_ones = nc.dram_tensor("onesin", [128, 4], F32, kind="ExternalInput").ap()
    d_sel16 = nc.dram_tensor("sel16", [16, 1024], F32, kind="ExternalInput").ap()

    s_x = A(nc.sbuf_tensor("s_x", [128, NCH * S], F32))
    s_xlnb = A(nc.sbuf_tensor("s_xlnb", [128, NCH * XL], BF16))
    s_ar2 = A(nc.sbuf_tensor("s_ar2", [128, 16384], BF16))
    s_scr = A(nc.sbuf_tensor("s_scr", [128, 16384], BF16))
    s_w = [A(nc.sbuf_tensor(f"s_w{i}", [128, 8192], BF16)) for i in range(3)]
    s_wfbdk = A(nc.sbuf_tensor("s_wfbdk", [128, 512], BF16))
    s_wfbdq = A(nc.sbuf_tensor("s_wfbdq", [128, 512], BF16))
    s_mu = A(nc.sbuf_tensor("s_mu", [128, 48], F32))
    s_ident = A(nc.sbuf_tensor("s_ident", [128, 128], BF16))
    s_m01 = A(nc.sbuf_tensor("s_m01", [128, 128], BF16))
    s_onesb = A(nc.sbuf_tensor("s_onesb", [128, 4], BF16))
    s_onesrow = A(nc.sbuf_tensor("s_onesrow", [1, 128], F32))
    s_sel16 = A(nc.sbuf_tensor("s_sel16", [16, 1024], F32))
    s_xbc = A(nc.sbuf_tensor("s_xbc", [128, 2048], BF16))
    xbcF = s_xbc[:, :].bitcast(F32)
    s_lntmp = xbcF[:, 0:512]
    s_rowA = xbcF[0:1, 0:512]
    s_rowC = xbcF[0:1, 512:1024]
    s_rn16 = s_scr[0:16, 14848:15872].bitcast(F32)
    s_rn8 = A(nc.sbuf_tensor("s_rn8", [128, 8], F32))
    s_cst = A(nc.sbuf_tensor("s_cst", [128, 4], F32))
    s_expb = A(nc.sbuf_tensor("s_expb", [128, 1024], BF16))
    s_v8 = A(nc.sbuf_tensor("s_v8", [128, 8 * 65], BF16))
    s_outx = A(nc.sbuf_tensor("s_outx", [128, 8 * 64], BF16))
    s_low = A(nc.sbuf_tensor("s_low", [128, 4096], BF16))

    PA = A(nc.psum_tensor("p_a", [128, 512], F32))
    PB = A(nc.psum_tensor("p_b", [128, 512], F32))
    KVA = A(nc.psum_tensor("p_kva", [128, 1024], F32))
    KVB = A(nc.psum_tensor("p_kvb", [128, 1024], F32))
    KVC = A(nc.psum_tensor("p_kvc", [128, 1024], F32))

    ctrs = {n: Ctr(A(nc.semaphore("c_" + n))) for n in ("t", "v", "a", "s", "g")}
    block = A(nc.Block())
    p = Prog(ctrs)

    def kv_slice(sl):
        t = KVA if sl < 16 else KVB
        o = 64 * (sl % 16)
        return t[:, o:o + 64]

    oKT, oKF, oVT = 0, 4096, 8192
    oQF, oO, oBD, oNT = 0, 8192, 10240, 14336
    oXR, oXK, oKA = 0, 4096, 8192
    oQ, oK, oV = 0, 4096, 8192

    def gdma(dst, src):
        p.op("gpsimd", lambda e, d=dst, s_=src: e.dma_start(out=d, in_=s_), inc="g", k=16)

    def sdma(dst, src):
        p.op("sync", lambda e, d=dst, s_=src: e.dma_start(out=d, in_=s_), inc="s", k=16)

    def mm(out, lhsT, rhs, start, stop, inc=None, tile_position=None):
        p.op("tensor",
             lambda e, o=out, l=lhsT, r=rhs, st=start, sp=stop, tp=tile_position:
             e.matmul(o, l, r, start=st, stop=sp, tile_position=tp,
                      skip_group_check=True),
             inc=inc)

    def ttrans(out, in_, ident, inc=None, tile_position=None):
        p.op("tensor",
             lambda e, o=out, i=in_, d=ident, tp=tile_position:
             e.transpose(o, i, d, tile_position=tp),
             inc=inc)

    def vop(fn):
        p.op("vector", fn, inc="v")

    def aop(fn):
        p.op("scalar", fn, inc="a")

    def tmark():
        mm(PA[0:1, 0:1], s_onesb[0:1, 0:1], s_onesb[0:1, 0:1], True, True, inc="t")
        return ctrs["t"].n

    def barrier():
        tn = tmark()
        vn, an = ctrs["v"].n, ctrs["a"].n
        p.wait("tensor", "v", vn)
        p.wait("tensor", "a", an)
        p.wait("vector", "t", tn)
        p.wait("vector", "a", an)
        p.wait("scalar", "t", tn)
        p.wait("scalar", "v", vn)

    gdma(s_mu[:], d_mu[:])
    gdma(s_ident[:], d_ident[:])
    gdma(s_m01[:], d_m01[:])
    gdma(s_onesb[:], d_ones[:])
    gdma(s_sel16[:], d_sel16[:])
    gdma(s_wfbdk[:], d_wfbdk[:])
    gdma(s_wfbdq[:], d_wfbdq[:])
    g_const = ctrs["g"].n
    p.wait("vector", "g", g_const)
    vop(lambda e: e.memset(s_onesrow[:], 1.0))
    vop(lambda e: e.memset(s_cst[:, 0:1], -10.0))
    vop(lambda e: e.memset(s_cst[:, 1:2], EPS))
    vop(lambda e: e.memset(
        s_v8.rearrange("p (g c) -> p g c", g=8)[:, :, 64:65], 1.0))
    ONESB = s_onesb[:, 0:1]

    slot_loaded = [0, 0, 0]
    slot_freed = [0, 0, 0]
    slot_rr = [0]

    def load_weight(dram_ap):
        slot = slot_rr[0]
        slot_rr[0] = (slot + 1) % 3
        p.wait("gpsimd", "t", slot_freed[slot])
        src = dram_ap.rearrange("(c p) f -> p c f", p=128)
        dst = s_w[slot].rearrange("p (c f) -> p c f", c=8)
        gdma(dst, src)
        slot_loaded[slot] = ctrs["g"].n
        return slot

    def free_slot(slot):
        slot_freed[slot] = ctrs["t"].n

    def wtile(slot, c, m):
        return s_w[slot][:, 1024 * c + 128 * m: 1024 * c + 128 * m + 128]

    def emit_ln(tile):
        t0 = tile * TT
        marks = {}
        for c in range(NCH):
            pp = 1024 * (c % 2)
            xb = s_xbc[:, pp: pp + 512]
            xq = s_xbc[:, pp + 512: pp + 1024]
            if c >= 2:
                p.wait("vector", "t", marks[c - 2])
            xs = s_x[:, S * c + t0: S * c + t0 + TT]
            vop(lambda e, xb=xb, xs=xs: e.tensor_copy(out=xb, in_=xs))
            vop(lambda e, xq=xq, xb=xb: e.tensor_mul(xq, xb, xb))
            p.wait("tensor", "v")
            mm(PA[0:1, :], ONESB, xb, start=(c == 0), stop=(c == NCH - 1))
            mm(PB[0:1, :], ONESB, xq, start=(c == 0), stop=(c == NCH - 1), inc="t")
            marks[c] = ctrs["t"].n
        p.wait("vector", "t")
        rA, rC = s_rowA, s_rowC
        vop(lambda e: e.tensor_scalar_mul(rA, PA[0:1, :], 1.0 / D))
        vop(lambda e: e.tensor_mul(rC, rA, rA))
        vop(lambda e: e.scalar_tensor_tensor(out=PA[0:1, :], in0=PB[0:1, :],
                                             scalar=1.0 / D, in1=rC,
                                             op0=ALU.mult, op1=ALU.subtract))
        p.wait("scalar", "v")
        aop(lambda e: e.activation(out=rC, in_=PA[0:1, :], func=AF.Sqrt,
                                   bias=s_cst[0:1, 1:2]))
        p.wait("vector", "a")
        vop(lambda e: e.reciprocal(out=rC, in_=rC))
        vop(lambda e: e.tensor_mul(rA, rA, rC))
        p.wait("tensor", "v")
        p.wait("tensor", "a")
        mm(PA[:], s_onesrow[:], rC, start=True, stop=True)
        mm(PB[:], s_onesrow[:], rA, start=True, stop=True, inc="t")
        p.wait("vector", "t")
        for c in range(NCH):
            xs = s_x[:, S * c + t0: S * c + t0 + TT]
            dst = s_xlnb[:, XL * c + 1: XL * c + 1 + TT]
            vop(lambda e, xs=xs: e.tensor_mul(s_lntmp, xs, PA[:]))
            vop(lambda e, dst=dst: e.tensor_sub(dst, s_lntmp, PB[:]))

    def xlnb_c(c, lo, n):
        return s_xlnb[:, XL * c + 1 + lo: XL * c + 1 + lo + n]

    class BankMarks:
        def __init__(self, n=2):
            self.m = [None] * n

        def wait(self, bank):
            mk = self.m[bank]
            if mk is None:
                p.wait("tensor", "v")
                p.wait("tensor", "a")
            else:
                p.wait("tensor", "v", mk[0])
                p.wait("tensor", "a", mk[1])

        def set(self, bank):
            self.m[bank] = (ctrs["v"].n, ctrs["a"].n)

    def emit_proj(slot, rhs_fn, evac, w=512):
        bk = BankMarks()
        for m in range(NCH):
            ps = (PA if m % 2 == 0 else PB)[:, 0:w]
            bk.wait(m % 2)
            for c in range(NCH):
                mm(ps, wtile(slot, c, m), rhs_fn(c), start=(c == 0), stop=(c == NCH - 1),
                   inc=("t" if c == NCH - 1 else None))
            evac(m, ps)
            bk.set(m % 2)

    def elu_evac(ps, dst, w=512):
        p.wait("vector", "t")
        t1 = s_xbc[:, 0:w]
        t3 = s_xbc[:, 1024:1024 + w]
        vop(lambda e, t1=t1, ps=ps: e.tensor_scalar_min(t1, ps, 0.0))
        p.wait("scalar", "t")
        p.wait("scalar", "v")
        aop(lambda e, t1=t1: e.activation(out=t1, in_=t1, func=AF.Exp))
        aop(lambda e, t3=t3, ps=ps: e.activation(out=t3, in_=ps, func=AF.Relu))
        p.wait("vector", "a")
        vop(lambda e, dst=dst, t1=t1, t3=t3: e.tensor_add(dst, t1, t3))

    def emit_attn_a():
        QS = s_ar2
        sq = load_weight(dw["wq"])
        sk = load_weight(dw["wk"])
        sv = load_weight(dw["wv"])
        so = [None]
        for tile in range(NTILES):
            emit_ln(tile)
            p.wait("tensor", "g", slot_loaded[sq])

            def ev_q(m, ps, tile=tile):
                p.wait("vector", "t")
                dst = QS[:, 2048 * m + TT * tile: 2048 * m + TT * tile + TT]
                vop(lambda e, dst=dst, ps=ps: e.tensor_copy(out=dst, in_=ps))
            emit_proj(sq, lambda c: xlnb_c(c, 0, TT), ev_q)
            if tile == NTILES - 1:
                free_slot(sq)
                so[0] = load_weight(dw["wo"])

            p.wait("tensor", "g", slot_loaded[sk])

            def ev_k(m, ps):
                p.wait("vector", "t")
                dst = s_scr[:, oKT + 512 * m: oKT + 512 * m + TT]
                vop(lambda e, dst=dst, ps=ps: e.tensor_copy(out=dst, in_=ps))
            emit_proj(sk, lambda c: xlnb_c(c, 0, TT), ev_k)
            if tile == NTILES - 1:
                free_slot(sk)
            p.wait("tensor", "g", slot_loaded[sv])

            for sub in range(NSUB):
                lo = 128 * sub
                p.wait("tensor", "v")
                for c in range(NCH):
                    lhsT = xlnb_c(c, lo, 128)
                    mm(PA[:], lhsT, s_w[sv][:, 1024 * c: 1024 * c + 512],
                       start=(c == 0), stop=(c == NCH - 1))
                    mm(PB[:], lhsT, s_w[sv][:, 1024 * c + 512: 1024 * c + 1024],
                       start=(c == 0), stop=(c == NCH - 1),
                       inc=("t" if c == NCH - 1 else None))
                p.wait("vector", "t")
                for half, ps in ((0, PA[:]), (1, PB[:])):
                    dst = s_scr[:, oVT + 512 * half: oVT + 512 * half + 512]
                    vop(lambda e, dst=dst, ps=ps: e.tensor_copy(out=dst, in_=ps))
                bkk = BankMarks()
                for c in range(NCH):
                    ps = (PA if c % 2 == 0 else PB)[:]
                    bkk.wait(c % 2)
                    mm(ps, s_scr[:, oKT + 512 * c + lo: oKT + 512 * c + lo + 128],
                       s_wfbdk[:], start=True, stop=True, inc="t")
                    elu_evac(ps, s_scr[:, oKF + 512 * c: oKF + 512 * c + 512])
                    bkk.set(c % 2)
                p.wait("tensor", "v")
                first = (tile == 0 and sub == 0)
                last = (tile == NTILES - 1 and sub == NSUB - 1)
                for h in range(H):
                    for ph in range(2):
                        sl = 2 * h + ph
                        o = oKF + 512 * (h // 2) + 256 * (h % 2) + 128 * ph
                        mm(kv_slice(sl), s_scr[:, o:o + 128],
                           s_scr[:, oVT + 64 * h: oVT + 64 * h + 64],
                           start=first, stop=last)
                        mm(KVC[:, 2 * sl:2 * sl + 1], s_scr[:, o:o + 128],
                           ONESB, start=first, stop=last,
                           inc=("t" if sl == 31 else None))
            barrier()
            if tile == NTILES - 1:
                free_slot(sv)

        p.wait("vector", "t")
        for c in range(NCH):
            for j in range(4):
                idx = 4 * c + j
                bd = s_scr[:, oBD + 128 * idx: oBD + 128 * idx + 128]
                ntl = s_scr[:, oNT + 16 * idx: oNT + 16 * idx + 16]
                vop(lambda e, bd=bd: e.memset(bd, 0.0))
                vop(lambda e, ntl=ntl: e.memset(ntl, 0.0))
                for par in range(2):
                    h = 2 * c + par
                    sl = 2 * h + (j // 2)
                    src = kv_slice(sl)[64 * (j % 2):64 * (j % 2) + 64, :]
                    ks = KVC[64 * (j % 2):64 * (j % 2) + 64, 2 * sl:2 * sl + 1]
                    vop(lambda e, bd=bd, src=src, par=par:
                        e.tensor_copy(out=bd[64 * par:64 * par + 64,
                                             64 * par:64 * par + 64],
                                      in_=src))
                    vop(lambda e, ntl=ntl, ks=ks, par=par, h=h:
                        e.tensor_copy(out=ntl[64 * par:64 * par + 64, h:h + 1],
                                      in_=ks))
        barrier()

        swo = so[0]
        HT = 256
        for ht in range(S // HT):
            tb = HT * ht
            bkq = BankMarks()
            for c in range(NCH):
                for j in range(4):
                    idx = 4 * c + j
                    ps = (PA if idx % 2 == 0 else PB)[:, 0:HT]
                    bkq.wait(idx % 2)
                    mm(ps, s_wfbdq[:, 128 * j:128 * j + 128],
                       QS[:, 2048 * c + tb: 2048 * c + tb + HT],
                       start=True, stop=True, inc="t")
                    elu_evac(ps, s_scr[:, oQF + 256 * idx: oQF + 256 * idx + HT], w=HT)
                    bkq.set(idx % 2)
            p.wait("tensor", "v")
            for idx in range(32):
                mm(KVA[0:16, 0:HT], s_scr[:, oNT + 16 * idx: oNT + 16 * idx + 16],
                   s_scr[:, oQF + 256 * idx: oQF + 256 * idx + HT],
                   start=(idx == 0), stop=(idx == 31),
                   inc=("t" if idx == 31 else None))
            p.wait("vector", "t")
            rn = s_rn16[:, 0:HT]
            vop(lambda e, rn=rn: e.tensor_scalar_add(rn, KVA[0:16, 0:HT], 1e-6))
            vop(lambda e, rn=rn: e.reciprocal(out=rn, in_=rn))
            for c in range(NCH):
                p.wait("tensor", "v")
                mm(KVC[:, 0:HT], s_sel16[:, 128 * c:128 * c + 128], s_rn16[:, 0:HT],
                   start=True, stop=True)
                for j in range(4):
                    idx = 4 * c + j
                    mm(KVB[:, 0:HT], s_scr[:, oBD + 128 * idx: oBD + 128 * idx + 128],
                       s_scr[:, oQF + 256 * idx: oQF + 256 * idx + HT],
                       start=(j == 0), stop=(j == 3), inc=("t" if j == 3 else None))
                p.wait("vector", "t")
                dst = s_scr[:, oO + 256 * c: oO + 256 * c + HT]
                vop(lambda e: e.tensor_copy(out=s_lntmp[:, 0:HT], in_=KVC[:, 0:HT]))
                vop(lambda e, dst=dst: e.tensor_mul(dst, KVB[:, 0:HT],
                                                    s_lntmp[:, 0:HT]))
            p.wait("tensor", "g", slot_loaded[swo])

            def ev_o(m, ps, tb=tb):
                p.wait("vector", "t")
                xs = s_x[:, S * m + tb: S * m + tb + HT]
                vop(lambda e, xs=xs, ps=ps: e.tensor_add(xs, xs, ps))
            emit_proj(swo, lambda c: s_scr[:, oO + 256 * c: oO + 256 * c + HT],
                      ev_o, w=HT)
            barrier()
        free_slot(swo)

    def emit_ffn(io_seq=None):
        fr = load_weight(dw["wr"])
        fk = load_weight(dw["wfk"])
        fv = load_weight(dw["wfv"])
        for tile in range(NTILES):
            emit_ln(tile)
            if tile == 0:
                for c in range(NCH):
                    vop(lambda e, c=c: e.memset(s_xlnb[:, XL * c:XL * c + 1], 0.0))
            def mu_shift(oDST, mo, omo):
                for c in range(NCH):
                    cur = s_xlnb[:, XL * c + 1: XL * c + 1 + TT]
                    prev = s_xlnb[:, XL * c: XL * c + TT]
                    t1 = s_xbc[:, 1024 * (c % 2): 1024 * (c % 2) + 512]
                    dst = s_scr[:, oDST + 512 * c: oDST + 512 * c + 512]
                    vop(lambda e, t1=t1, prev=prev, c=c, o=omo:
                        e.tensor_scalar_mul(t1, prev, s_mu[:, o + c:o + c + 1]))
                    vop(lambda e, dst=dst, cur=cur, t1=t1, c=c, o=mo:
                        e.scalar_tensor_tensor(out=dst, in0=cur,
                                               scalar=s_mu[:, o + c:o + c + 1],
                                               in1=t1, op0=ALU.mult, op1=ALU.add))
            mu_shift(oXR, 0, 8)
            p.wait("tensor", "g", slot_loaded[fr])

            def ev_r(m, ps):
                p.wait("scalar", "t")
                dst = s_ar2[:, 1024 * m: 1024 * m + 1024].bitcast(F32)
                aop(lambda e, dst=dst, ps=ps: e.activation(out=dst, in_=ps,
                                                           func=AF.Sigmoid))
            emit_proj(fr, lambda c: s_scr[:, oXR + 512 * c: oXR + 512 * c + 512], ev_r)
            if tile == NTILES - 1:
                free_slot(fr)
            mu_shift(oXK, 16, 24)
            for c in range(NCH):
                vop(lambda e, c=c: e.tensor_copy(
                    out=s_xlnb[:, XL * c:XL * c + 1],
                    in_=s_xlnb[:, XL * c + TT:XL * c + TT + 1]))
            p.wait("tensor", "g", slot_loaded[fk])

            def ev_k(m, ps):
                p.wait("scalar", "t")
                p.wait("scalar", "v")
                t1 = s_xbc[:, 1024 * (m % 2): 1024 * (m % 2) + 512]
                dst = s_scr[:, oKA + 512 * m: oKA + 512 * m + 512]
                aop(lambda e, t1=t1, ps=ps: e.activation(out=t1, in_=ps, func=AF.Relu))
                p.wait("vector", "a")
                vop(lambda e, dst=dst, t1=t1: e.tensor_mul(dst, t1, t1))
            emit_proj(fk, lambda c: s_scr[:, oXK + 512 * c: oXK + 512 * c + 512], ev_k)
            if tile == NTILES - 1:
                free_slot(fk)
            p.wait("tensor", "g", slot_loaded[fv])
            t0 = tile * TT

            def ev_v(m, ps, t0=t0, tile=tile):
                p.wait("vector", "t")
                p.wait("vector", "a")
                r = s_ar2[:, 1024 * m: 1024 * m + 1024].bitcast(F32)
                xs = s_x[:, S * m + t0: S * m + t0 + TT]
                vop(lambda e, r=r, ps=ps: e.tensor_mul(s_lntmp, r, ps))
                vop(lambda e, xs=xs: e.tensor_add(xs, xs, s_lntmp))
                if io_seq is not None and tile == NTILES - 1:
                    p.wait("sync", "v")
                    ov = d_out[io_seq].rearrange("(c p) t -> p c t", p=128)
                    sdma(ov[:, m, :], s_x[:, S * m: S * m + S])
                    if io_seq + 1 < SEQ_PER_CORE:
                        p.wait("sync", "s")
                        xv = d_x[io_seq + 1].rearrange("(c p) t -> p c t", p=128)
                        sdma(s_x[:, S * m: S * m + S], xv[:, m, :])
            emit_proj(fv, lambda c: s_scr[:, oKA + 512 * c: oKA + 512 * c + 512], ev_v)
            barrier()
            if tile == NTILES - 1:
                free_slot(fv)

    def emit_attn_b(seqidx=0):
        OD = s_ar2
        bq = load_weight(dw["wqkv"][:, 0:D])
        bk = load_weight(dw["wqkv"][:, D:2 * D])
        bv = load_weight(dw["wqkv"][:, 2 * D:3 * D])
        bslots = (bq, bk, bv)
        wp = [None]
        qlow = s_low[0:64, 0:2048]
        klow = s_low[0:64, 2048:4096]
        vlow = s_low[64:128, 0:2048]
        for tile in range(NTILES):
            emit_ln(tile)
            p.wait("vector", "s")
            for ki, oDst in ((0, oQ), (1, oK), (2, oV)):
                kslot = bslots[ki]
                p.wait("tensor", "g", slot_loaded[kslot])
                reg = s_scr[:, oDst:oDst + 4096].rearrange("p (g w) -> p g w", g=64)

                def ev(m, ps, reg=reg):
                    p.wait("vector", "t")
                    dst = reg[:, :, 8 * m:8 * m + 8]
                    psv = ps.rearrange("p (g t) -> p g t", g=64)
                    vop(lambda e, dst=dst, psv=psv: e.tensor_copy(out=dst, in_=psv))
                emit_proj(kslot, lambda c: xlnb_c(c, 0, TT), ev)
                if tile == NTILES - 1:
                    free_slot(kslot)
                    if ki == 0:
                        wp[0] = load_weight(dw["wproj"])
            for sub in range(NSUB):
                lo = 128 * sub
                p.wait("sync", "v")
                p.wait("sync", "t")
                for oSrc, lowt in ((oQ, qlow), (oK, klow), (oV, vlow)):
                    reg = s_scr[:, oSrc:oSrc + 4096].rearrange("p (g w) -> p g w", g=64)
                    se = reg[0:64, 16 * sub:16 * sub + 16, :]
                    so = reg[64:128, 16 * sub:16 * sub + 16, :]
                    lv = lowt.rearrange("p (g par w) -> p g par w", g=16, par=2)
                    sdma(lv[:, :, 0, :], se)
                    sdma(lv[:, :, 1, :], so)
                p.wait("tensor", "s")
                for gb in range(2):
                    p.wait("tensor", "a")
                    for g8 in range(8):
                        grp = 8 * gb + g8
                        mm(KVC[:, 128 * g8:128 * g8 + 128],
                           klow[:, 128 * grp:128 * grp + 128],
                           qlow[:, 128 * grp:128 * grp + 128],
                           start=True, stop=True, inc=("t" if g8 == 7 else None))
                    p.wait("scalar", "t")
                    p.wait("scalar", "v")
                    aop(lambda e: e.activation(out=s_expb[:], in_=KVC[:, 0:1024],
                                               func=AF.Exp, bias=s_cst[:, 0:1], scale=0.125))
                    p.wait("vector", "a")
                    for g8 in range(8):
                        ev_ = s_expb[:, 128 * g8:128 * g8 + 128]
                        vop(lambda e, ev_=ev_: e.tensor_tensor(
                            out=ev_, in0=ev_, in1=s_m01[:], op=ALU.mult))
                    if ATTNB_STAGE < 2:
                        continue
                    p.wait("tensor", "v")
                    for g8 in range(8):
                        grp = 8 * gb + g8
                        pslot = PA[:, 32 * g8:32 * g8 + 32].bitcast(BF16)
                        ttrans(pslot, vlow[:, 128 * grp:128 * grp + 128],
                               s_ident[64:128, 64:128],
                               tile_position=(64, 0),
                               inc=("t" if g8 == 7 else None))
                    p.wait("scalar", "t")
                    for g8 in range(8):
                        pslot = PA[:, 32 * g8:32 * g8 + 32].bitcast(BF16)
                        dst = s_v8[:, 65 * g8:65 * g8 + 64]
                        aop(lambda e, dst=dst, pslot=pslot: e.activation(
                            out=dst, in_=pslot, func=AF.Copy))
                    if ATTNB_STAGE < 3:
                        continue
                    p.wait("tensor", "a")
                    p.wait("tensor", "v")
                    for g8 in range(8):
                        mm(KVA[:, 128 * g8:128 * g8 + 65],
                           s_expb[:, 128 * g8:128 * g8 + 128],
                           s_v8[:, 65 * g8:65 * g8 + 65],
                           start=True, stop=True, inc=("t" if g8 == 7 else None))
                    p.wait("vector", "t")
                    kvv = KVA[:, 0:1024].rearrange("p (g c) -> p g c", g=8)
                    vop(lambda e, kvv=kvv: e.reciprocal(out=s_rn8[:],
                                                        in_=kvv[:, :, 64:65]))
                    p.wait("vector", "v")
                    for g8 in range(8):
                        ov = s_outx[:, 64 * g8:64 * g8 + 64]
                        kslc = KVA[:, 128 * g8:128 * g8 + 64]
                        vop(lambda e, ov=ov, kslc=kslc, g8=g8: e.tensor_scalar_mul(
                            ov, kslc, s_rn8[:, g8:g8 + 1]))
                    if ATTNB_STAGE < 4:
                        continue
                    p.wait("tensor", "v")
                    for g8 in range(8):
                        pslot = PA[:, 256 + 32 * g8:256 + 32 * g8 + 32].bitcast(BF16)
                        ine = s_outx[0:64, 64 * g8:64 * g8 + 64]
                        ino = s_outx[64:128, 64 * g8:64 * g8 + 64]
                        ttrans(pslot[0:64, :], ine, s_ident[0:64, 0:64])
                        ttrans(pslot[64:128, :], ino, s_ident[64:128, 64:128],
                               tile_position=(64, 64),
                               inc=("t" if g8 == 7 else None))
                    p.wait("scalar", "t")
                    gbase = (tile * TT + lo) // 8 + 8 * gb
                    dst = OD[:, 64 * gbase: 64 * gbase + 512]
                    psrc = PA[:, 256:512].bitcast(BF16)
                    aop(lambda e, dst=dst, psrc=psrc: e.activation(
                        out=dst, in_=psrc, func=AF.Copy))
                    if DBG and seqidx == 0 and tile == 0 and sub == 0 and gb == 0:
                        dbf = s_scr[:, 12288:14336]
                        dbF = dbf.bitcast(F32)
                        p.wait("vector", "t")
                        p.wait("vector", "a")
                        vop(lambda e: e.tensor_copy(out=dbF[:, 0:512],
                                                    in_=KVC[:, 0:512]))
                        vop(lambda e: e.tensor_copy(out=dbF[:, 512:640],
                                                    in_=s_expb[:, 0:128]))
                        vop(lambda e: e.tensor_copy(out=dbF[:, 640:705],
                                                    in_=s_v8[:, 0:65]))
                        vop(lambda e: e.tensor_copy(out=dbF[:, 705:769],
                                                    in_=s_outx[:, 0:64]))
                        vop(lambda e: e.tensor_copy(out=dbF[:, 769:834],
                                                    in_=KVA[:, 0:65]))
                        vop(lambda e: e.tensor_copy(out=dbF[:, 834:898],
                                                    in_=OD[:, 0:64]))
                        p.wait("sync", "v")
                        dv = d_out[0].rearrange("(c p) t -> p c t", p=128)
                        sdma(dv[:, 0, 0:1024], dbF[:, 0:1024])
            barrier()
        if ATTNB_STAGE < 4:
            return
        swp = wp[0]
        p.wait("tensor", "g", slot_loaded[swp])
        odv = OD.rearrange("p (G c) -> p G c", G=256)
        bkp = BankMarks()
        for tile in range(NTILES):
            t0 = tile * TT
            for m in range(NCH):
                idx = (tile * NCH + m) % 2
                PP = KVB if idx == 0 else KVA
                bkp.wait(idx)
                for hp in range(8):
                    le = s_w[swp][0:64, 1024 * hp + 128 * m: 1024 * hp + 128 * m + 128]
                    lo_ = s_w[swp][64:128, 1024 * hp + 128 * m: 1024 * hp + 128 * m + 128]
                    re = odv[0:64, 64 * tile:64 * tile + 64, 8 * hp:8 * hp + 8]
                    ro = odv[64:128, 64 * tile:64 * tile + 64, 8 * hp:8 * hp + 8]
                    mm(PP[:, 0:512], le, re, start=(hp == 0), stop=(hp == 7))
                    mm(PP[:, 512:1024], lo_, ro, start=(hp == 0), stop=(hp == 7),
                       inc=("t" if hp == 7 else None))
                p.wait("vector", "t")
                xs = s_x[:, S * m + t0: S * m + t0 + TT]
                bcol = s_mu[:, 32 + m:33 + m]
                vop(lambda e, xs=xs, bcol=bcol, PP=PP: e.scalar_tensor_tensor(
                    out=s_lntmp, in0=PP[:, 512:1024], scalar=bcol,
                    in1=xs, op0=ALU.add, op1=ALU.add))
                vop(lambda e, xs=xs, PP=PP: e.tensor_add(xs, s_lntmp, PP[:, 0:512]))
                bkp.set(idx)
        barrier()
        free_slot(swp)

    for seq in range(1 if DBG else SEQ_PER_CORE):
        if seq == 0:
            xv = d_x[0].rearrange("(c p) t -> p c t", p=128)
            for c in range(NCH):
                sdma(s_x[:, S * c: S * c + S], xv[:, c, :])
        p.wait("tensor", "s")
        p.wait("vector", "s")
        p.wait("scalar", "s")
        if 0 in SUBLAYERS:
            emit_attn_a()
        if 1 in SUBLAYERS:
            emit_ffn()
        if 2 in SUBLAYERS:
            emit_attn_b(seq)
        if 3 in SUBLAYERS:
            emit_ffn(io_seq=seq)
        barrier()
    p.wait("sync", "s")
    fin = ctrs["s"].n
    p.wait("gpsimd", "s", fin)

    @block.tensor
    def _(e):
        for fn in p.streams["tensor"]:
            fn(e)

    @block.vector
    def _(e):
        for fn in p.streams["vector"]:
            fn(e)

    @block.scalar
    def _(e):
        for fn in p.streams["scalar"]:
            fn(e)

    @block.sync
    def _(e):
        for fn in p.streams["sync"]:
            fn(e)

    @block.gpsimd
    def _(e):
        for fn in p.streams["gpsimd"]:
            fn(e)

    es.close()
    nc.finalize()
    _CACHE["nc"] = nc
    return nc



def _host_inputs(inputs):
    x = np.asarray(inputs["L0_attn_a_hidden_states"], dtype=np.float32)
    xt = np.ascontiguousarray(x.transpose(0, 2, 1))

    def T(a):
        return np.ascontiguousarray(
            np.asarray(a, dtype=np.float32).T).astype(ml_dtypes.bfloat16)

    wq = T(inputs["L0_attn_a_Wq"])
    wk = T(inputs["L0_attn_a_Wk"])
    wv = T(inputs["L0_attn_a_Wv"])
    wo = T(inputs["L0_attn_a_Wo"])
    wr = T(inputs["L0_ffn_W_r"])
    wfk = T(inputs["L0_ffn_W_k"])
    wfv = T(inputs["L0_ffn_W_v"])
    wqkv_raw = T(inputs["L0_attn_b_Wqkv"])
    perm = np.concatenate([np.concatenate([192 * h + off + np.arange(64)
                                           for h in range(16)])
                           for off in (0, 64, 128)])
    wqkv = np.ascontiguousarray(wqkv_raw[:, perm])
    wproj = T(inputs["L0_attn_b_Wproj"])
    wfT = T(inputs["L0_attn_a_Wf"])

    wfbdk = np.zeros((128, 512), dtype=np.float32)
    wfbdk[0:64, 0:256] = wfT
    wfbdk[64:128, 256:512] = wfT
    wfbdq = np.zeros((128, 512), dtype=np.float32)
    for j in range(4):
        wfbdq[0:64, 128 * j:128 * j + 64] = wfT[:, 64 * j:64 * j + 64]
        wfbdq[64:128, 128 * j + 64:128 * j + 128] = wfT[:, 64 * j:64 * j + 64]

    mu = np.zeros((128, 48), dtype=np.float32)
    mur = np.asarray(inputs["L0_ffn_mu_r"], dtype=np.float32).reshape(8, 128).T
    muk = np.asarray(inputs["L0_ffn_mu_k"], dtype=np.float32).reshape(8, 128).T
    bpr = np.asarray(inputs["L0_attn_b_bproj"], dtype=np.float32).reshape(8, 128).T
    mu[:, 0:8] = mur
    mu[:, 8:16] = 1.0 - mur
    mu[:, 16:24] = muk
    mu[:, 24:32] = 1.0 - muk
    mu[:, 32:40] = bpr

    ident = np.eye(128, dtype=ml_dtypes.bfloat16)
    pp, nn = np.meshgrid(np.arange(128), np.arange(128), indexing="ij")
    m01 = ((nn % 8) == (pp % 8)).astype(ml_dtypes.bfloat16)
    ones = np.ones((128, 4), dtype=np.float32)
    sel16 = np.zeros((16, 1024), dtype=np.float32)
    for c in range(8):
        sel16[2 * c, 128 * c:128 * c + 64] = 1.0
        sel16[2 * c + 1, 128 * c + 64:128 * c + 128] = 1.0

    shared = dict(wq=wq, wk=wk, wv=wv, wo=wo, wr=wr, wfk=wfk, wfv=wfv,
                  wqkv=wqkv, wproj=wproj,
                  wfbdk=wfbdk.astype(ml_dtypes.bfloat16),
                  wfbdq=wfbdq.astype(ml_dtypes.bfloat16), mu=mu,
                  ident=ident, m01=m01, onesin=ones, sel16=sel16)
    in_maps = []
    for core in range(NCORES):
        m = dict(shared)
        m["x"] = np.ascontiguousarray(xt[core * SEQ_PER_CORE:(core + 1) * SEQ_PER_CORE])
        in_maps.append(m)
    return in_maps


def kernel(**inputs):
    nc = build_program()
    in_maps = _host_inputs(inputs)
    res = run_bass_kernel_spmd(nc, in_maps, core_ids=list(range(NCORES)))
    outs = [res.results[c]["out"] for c in range(NCORES)]
    full = np.concatenate(outs, axis=0)
    return np.ascontiguousarray(full.transpose(0, 2, 1))
